# version 07
# baseline (speedup 1.0000x reference)
import math
import numpy as np
import ml_dtypes
from contextlib import ExitStack

import concourse.bass as bass
import concourse.mybir as mybir
from concourse.bass_utils import run_bass_kernel_spmd

F32 = mybir.dt.float32
BF16 = mybir.dt.bfloat16
I32 = mybir.dt.int32
U32INT = mybir.dt.uint32
AF = mybir.ActivationFunctionType
ALU = mybir.AluOpType
AXX = mybir.AxisListType.X

NC = 8
S = 8192
DM = 4096
H = 32
KVH = 8
D = 128
HD2 = 64
INTER = 11008
ISL = INTER // NC
TOWN = S // NC
THALF = TOWN // 2
B_EXP = 25.0
THETA = 10000.0

IW = [128] * 10 + [96]
IOFF = [sum(IW[:i]) for i in range(len(IW))]
NIC = len(IW)


def _bf(x):
    return np.ascontiguousarray(np.asarray(x).astype(ml_dtypes.bfloat16))


def _host_consts(c):
    own_abs = 8 * np.arange(TOWN) + c
    inv = 1.0 / THETA ** (np.arange(HD2, dtype=np.float64) / HD2)
    ang = own_abs[:, None].astype(np.float64) * inv[None, :]
    fq = 1.0 / (4.0 * math.sqrt(D))
    out = {
        "ropeq_cos": np.ascontiguousarray((np.cos(ang).T * fq).astype(np.float32)),
        "ropeq_sin": np.ascontiguousarray((np.sin(ang).T * fq).astype(np.float32)),
        "ropek_cos_t": np.ascontiguousarray(np.cos(ang).astype(np.float32)),
        "ropek_sin_t": np.ascontiguousarray(np.sin(ang).astype(np.float32)),
        "identm": np.eye(128, dtype=ml_dtypes.bfloat16),
        "iota_col": np.arange(128, dtype=np.float32).reshape(128, 1),
        "nibm": (np.uint32(15) << (4 * (np.arange(128, dtype=np.uint32) % 8)))
                .astype(np.uint32).view(np.int32).reshape(128, 1),
        "nibp": (2.0 ** (-4.0 * (np.arange(128) % 8))).astype(np.float32).reshape(128, 1),
    }
    k_ = np.arange(DM)
    out["fp4oh"] = _bf((k_[None, :] // 128 == np.arange(32)[:, None]))
    masks = np.zeros((128, 8 * 512), dtype=np.float32)
    ki = np.arange(128)
    absoff = 8 * (ki % 16) + ki // 16
    qi = np.arange(128)
    for d in range(8):
        valid = ((8 * qi[None, :] + c) >= (128 * d + absoff[:, None])).astype(np.float32)
        masks[:, 512 * d:512 * (d + 1)] = np.tile(valid, (1, 4))
    out["dmasks"] = _bf(masks)
    return out


def build(upto=99, nogptq=False, nocoll=False, dbg=False):
    nc = bass.Bass("TRN2", target_bir_lowering=False, debug=False)

    def din(name, shape, dt):
        return nc.dram_tensor(name, list(shape), dt, kind="ExternalInput").ap()

    X = din("x_own", (TOWN, DM), F32)
    QC = din("q_codes", (DM, DM), I32)
    KC = din("k_codes", (DM, KVH * D), I32)
    VC = din("v_codes", (DM, KVH * D), I32)
    OC = din("o_codes", (DM, DM), I32)
    QS = din("q_scales", (32, DM), F32)
    KS = din("k_scales", (32, KVH * D), F32)
    VS = din("v_scales", (32, KVH * D), F32)
    OS = din("o_scales", (32, DM), F32)
    GQW = din("g_qw", (DM // 8, ISL), I32)
    GSC = din("g_sc", (32, ISL), F32)
    GQZ = din("g_qz", (32, ISL // 8), I32)
    GGI = din("g_gi", (DM,), I32)
    UQW = din("u_qw", (DM // 8, ISL), I32)
    USC = din("u_sc", (32, ISL), F32)
    UQZ = din("u_qz", (32, ISL // 8), I32)
    UGI = din("u_gi", (DM,), I32)
    DQW = din("d_qw", (ISL // 8, DM), I32)
    DSC = din("d_sc", (86, DM), F32)
    DQZ = din("d_qz", (86, DM // 8), I32)
    DGI = din("d_gi", (ISL,), I32)
    RQC = din("ropeq_cos", (HD2, TOWN), F32)
    RQS = din("ropeq_sin", (HD2, TOWN), F32)
    RKC = din("ropek_cos_t", (TOWN, HD2), F32)
    RKS = din("ropek_sin_t", (TOWN, HD2), F32)
    IDM = din("identm", (128, 128), BF16)
    IOTA = din("iota_col", (128, 1), F32)
    NIBM = din("nibm", (128, 1), I32)
    NIBP = din("nibp", (128, 1), F32)
    FP4OH = din("fp4oh", (32, DM), BF16)
    DMASK = din("dmasks", (128, 8 * 512), BF16)

    OUT = nc.dram_tensor("out_own", [TOWN, DM], F32, kind="ExternalOutput").ap()

    KD = "ExternalOutput" if dbg else "Internal"
    q_store = nc.dram_tensor("q_store", [H * D, TOWN], BF16, kind=KD).ap()
    kin = nc.dram_tensor("kin", [TOWN, KVH * D], BF16).ap()
    vin = nc.dram_tensor("vin", [TOWN, KVH * D], BF16).ap()
    kout = [nc.dram_tensor(f"kout{r}", [NC * THALF, KVH * D], BF16).ap() for r in range(2)]
    vout = [nc.dram_tensor(f"vout{r}", [NC * THALF, KVH * D], BF16).ap() for r in range(2)]
    wg_sh = nc.dram_tensor("wg_sh", [DM, ISL], BF16).ap()
    wu_sh = nc.dram_tensor("wu_sh", [DM, ISL], BF16).ap()
    wd_sh = nc.dram_tensor("wd_sh", [ISL, DM], BF16).ap()
    wgF = nc.dram_tensor("wgF", [NC * DM, ISL], BF16).ap()
    wuF = nc.dram_tensor("wuF", [NC * DM, ISL], BF16).ap()
    wdF = nc.dram_tensor("wdF", [INTER, DM], BF16).ap()
    ao_store = nc.dram_tensor("ao_store", [TOWN, H * D], BF16, kind=KD).ap()
    s_store = nc.dram_tensor("s_store", [TOWN, DM], BF16, kind=KD).ap()
    if dbg:
        dbg_outs = []
        for nm, srcap in [("dbg_kin", kin), ("dbg_vin", vin), ("dbg_kout0", kout[0]),
                          ("dbg_kout1", kout[1]), ("dbg_vout0", vout[0]),
                          ("dbg_vout1", vout[1]),
                          ("dbg_wg_sh", wg_sh), ("dbg_wd_sh", wd_sh)]:
            o = nc.dram_tensor(nm, list(srcap.shape), BF16, kind="ExternalOutput").ap()
            dbg_outs.append((o, srcap))
        dbg_wgf = nc.dram_tensor("dbg_wgF0", [DM, ISL], BF16, kind="ExternalOutput").ap()
        dbg_xnfm = nc.dram_tensor("dbg_xnfm", [128, 32 * TOWN], BF16, kind="ExternalOutput").ap()
        dbg_xnt = nc.dram_tensor("dbg_xnt", [TOWN, DM], BF16, kind="ExternalOutput").ap()
        dbg_sts = nc.dram_tensor("dbg_sts", [128, 512], F32, kind="ExternalOutput").ap()
        dbg_eps = nc.dram_tensor("dbg_eps", [128, 16], F32, kind="ExternalOutput").ap()

    ENGS = ("sync", "gpsimd", "vector", "scalar", "tensor")
    prog = {e: [] for e in ENGS}
    ctr = {}
    sems = {}
    es = ExitStack()
    LDL = [f"LD{i}" for i in range(16)]
    STL = [f"ST{i}" for i in range(8)]
    GPL = [f"GP{i}" for i in range(4)]
    for sname in LDL + STL + GPL + ["PE", "VE", "AC", "PO"]:
        sems[sname] = es.enter_context(nc.semaphore(sname))
        ctr[sname] = 0
    lane_rr = {"LD": 0, "ST": 0, "GP": 0}
    LANES = {"LD": LDL, "ST": STL, "GP": GPL}
    CCNAMES = ["cK0", "cV0", "cK1", "cV1", "cWG", "cWU", "cWD"]
    for sname in CCNAMES:
        sems[sname] = es.enter_context(nc.semaphore(sname))
        ctr[sname] = 0

    def op(eng, fn, inc=None, waits=(), inck=1):
        tok = None
        if inc is not None:
            ctr[inc] += inck
            tok = ctr[inc]
        wl = tuple((sems[s], v) for (s, v) in [w for w in waits if w is not None]
                   if v and v > 0)

        def run(e, fn=fn, wl=wl, inc=inc, inck=inck):
            for s, v in wl:
                e.wait_ge(s, v)
            inst = fn(e)
            if inc is not None:
                inst.then_inc(sems[inc], inck)
        prog[eng].append(run)
        return tok

    def wend(eng, waits):
        wl = tuple((sems[s], v) for (s, v) in [w for w in waits if w is not None]
                   if v and v > 0)

        def run(e, wl=wl):
            for s, v in wl:
                e.wait_ge(s, v)
        prog[eng].append(run)

    def dma(eng, out_ap, in_ap, waits=(), sem=None):
        q = sem or {"scalar": "ST", "gpsimd": "GP"}.get(eng, "LD")
        lanes = LANES[q]
        lane = lanes[lane_rr[q] % len(lanes)]
        lane_rr[q] += 1
        waits = tuple(waits) + ((lane, ctr[lane]),)
        v = op(eng, lambda e, o=out_ap, i=in_ap: e.dma_start(out=o, in_=i),
               inc=lane, waits=waits, inck=16)
        return (lane, v)

    def dma_now(q):
        return [(ln, ctr[ln]) for ln in LANES[q] if ctr[ln] > 0]

    def dma_op(eng, fn, waits=(), q="LD"):
        lanes = LANES[q]
        lane = lanes[lane_rr[q] % len(lanes)]
        lane_rr[q] += 1
        waits = tuple(waits) + ((lane, ctr[lane]),)
        v = op(eng, fn, inc=lane, waits=waits, inck=16)
        return (lane, v)

    def barrier():
        now = [(s, ctr[s]) for s in sems if not s.startswith("c") and ctr.get(s, 0) > 0]
        for eng in ENGS:
            wend(eng, now)

    sb_guards = {}
    sb_seq = [0]

    def sbuf(name, shape, dt):
        g = nc.sbuf_tensor(name, list(shape), dt)
        t = g.__enter__()
        sb_seq[0] += 1
        sb_guards[name] = (t, g, sb_seq[0])
        return t

    def sfree(*names):
        for n in sorted(names, key=lambda n: -sb_guards[n][2]):
            sb_guards.pop(n)[1].__exit__(None, None, None)

    psA = es.enter_context(nc.psum_tensor("psA", [128, 1024], F32))
    psB = es.enter_context(nc.psum_tensor("psB", [128, 1024], F32))
    ps2 = es.enter_context(nc.psum_tensor("ps2", [128, 512], F32))
    ps3 = es.enter_context(nc.psum_tensor("ps3", [128, 512], F32))
    ps4 = es.enter_context(nc.psum_tensor("ps4", [128, 512], F32))
    ps5 = es.enter_context(nc.psum_tensor("ps5", [128, 512], F32))
    PSW = [ps2, ps3, ps4, ps5]

    xn_fm = sbuf("xn_fm", [128, 32 * TOWN], BF16)
    ident = sbuf("ident", [128, 128], BF16)
    iota_c = sbuf("iota_c", [128, 1], F32)
    junkc = sbuf("junkc", [128, 1], F32)
    fp4oh = sbuf("fp4oh_t", [32, DM], BF16)
    negB = sbuf("negB", [128, 1], F32)
    epsT = sbuf("epsT", [128, 1], F32)
    onesG = sbuf("onesG", [1, 128], BF16)
    nibm = sbuf("nibm_t", [128, 1], I32)
    nibp = sbuf("nibp_t", [128, 1], F32)
    rqc = sbuf("rqc", [HD2, TOWN], F32)
    rqs = sbuf("rqs", [HD2, TOWN], F32)
    rkc = sbuf("rkc", [128, 8 * HD2], F32)
    rks = sbuf("rks", [128, 8 * HD2], F32)
    s4 = {"q": sbuf("s4q", [32, DM], BF16), "k": sbuf("s4k", [32, KVH * D], BF16),
          "v": sbuf("s4v", [32, KVH * D], BF16), "o": sbuf("s4o", [32, DM], BF16)}

    t_consts = 0
    for dst, src in [(ident[:, :], IDM), (iota_c[:, :], IOTA), (fp4oh[:, :], FP4OH),
                     (rqc[:, :], RQC), (rqs[:, :], RQS),
                     (nibm[:, :], NIBM), (nibp[:, :], NIBP)]:
        t_consts = dma("sync", dst, src)
    dma("sync", rkc[:, :], RKC.rearrange("(tt p) h -> p tt h", p=128))
    dma("sync", rks[:, :], RKS.rearrange("(tt p) h -> p tt h", p=128))
    import os as _os
    _notab = bool(int(_os.environ.get("KNOTAB", "0")))
    t_s4tok = {}
    for key, src in ([] if _notab else [("q", QS), ("k", KS), ("v", VS), ("o", OS)]):
        t_s4tok[key] = dma("gpsimd", s4[key][:, :], src)
    op("vector", lambda e: e.memset(negB[:, :], -B_EXP))
    op("vector", lambda e: e.memset(epsT[:, :], 1e-5))
    t_gp_init = op("vector", lambda e: e.memset(onesG[:, :], 1.0), inc="VE")

    sgt = sbuf("sgt", [32, ISL], BF16)
    sut = sbuf("sut", [32, ISL], BF16)
    sdt = sbuf("sdt", [86, DM], BF16)
    zgt = sbuf("zgt", [32, ISL], BF16)
    zut = sbuf("zut", [32, ISL], BF16)
    zdt = sbuf("zdt", [86, DM], BF16)
    if not _notab:
        t_sg = dma("gpsimd", sgt[:, :], GSC)
        t_su = dma("gpsimd", sut[:, :], USC)
        t_sd = dma("gpsimd", sdt[:, :], DSC)
    zraw = sbuf("zraw", [86, DM // 8], I32)
    zint = sbuf("zint", [86, DM], I32)
    t_zg = dma("sync", zraw[0:32, 0:ISL // 8], GQZ) if not _notab else None

    def emit_zp(raw, G, NP, ztab, tld):
        for s8 in range(8):
            op("vector", lambda e, r=raw, s8=s8, G=G, NP=NP:
               e.tensor_scalar(out=zint[0:G, 0:8 * NP][:, s8::8],
                               in0=r[0:G, 0:NP],
                               scalar1=4 * s8, scalar2=15,
                               op0=ALU.logical_shift_right, op1=ALU.bitwise_and),
               waits=[tld] if s8 == 0 else ())
        return op("vector", lambda e, zt=ztab, G=G, NP=NP:
                  e.tensor_scalar_add(zt[0:G, 0:8 * NP], zint[0:G, 0:8 * NP], 1.0), inc="VE")

    if not _notab:
        t_zgt = emit_zp(zraw, 32, ISL // 8, zgt, t_zg)
        t_zu = dma("sync", zraw[0:32, 0:ISL // 8], UQZ, waits=[("VE", t_zgt)])
        t_zut = emit_zp(zraw, 32, ISL // 8, zut, t_zu)
        t_zd = dma("sync", zraw[:, :], DQZ, waits=[("VE", t_zut)])
        t_zdt = emit_zp(zraw, 86, DM // 8, zdt, t_zd)
    else:
        t_zdt = 0
    sfree("zint")

    xbuf = [sbuf(f"xbuf{i}", [128, DM], F32) for i in range(2)]
    xnt = [sbuf(f"xnt{i}", [128, DM], BF16) for i in range(2)]
    sqd = sbuf("sqd", [128, DM], F32)
    st_s = sbuf("st_s", [128, 64], F32)
    dbgep = sbuf("dbgep", [128, 16], F32) if dbg else None
    t_xn_cons = [0, 0]
    t_ev_last = [0]
    t_xnt_pe = [0, 0]
    t_stats_prev = 0

    XR = X.rearrange("(tt p) n -> tt p n", p=128)
    t_x = [dma("sync", xbuf[0][:, :], XR[0], waits=[("VE", t_zdt)]),
           dma("sync", xbuf[1][:, :], XR[1], waits=[("VE", t_zdt)])]
    for t8 in range(8):
        xb, xn = xbuf[t8 % 2], xnt[t8 % 2]
        tld = t_x[t8]
        op("vector", lambda e, xb=xb: e.reduce_sum(st_s[:, 0:1], xb[:, :], axis=AXX),
           waits=[tld])
        op("vector", lambda e, xb=xb: e.tensor_tensor(
            out=sqd[:, :], in0=xb[:, :], in1=xb[:, :], op=ALU.mult))
        op("vector", lambda e: e.reduce_sum(st_s[:, 8:9], sqd[:, :], axis=AXX))
        spc = lambda e: e.tensor_scalar_mul(st_s[:, 62:63], iota_c[:, :], 1.0)
        aspc = lambda e: e.activation(out=st_s[:, 61:62], in_=epsT[:, :], func=AF.Copy)
        op("vector", lambda e: e.tensor_scalar_mul(st_s[:, 16:17], st_s[:, 0:1], 1.0 / DM))
        op("vector", lambda e: e.tensor_scalar_mul(st_s[:, 24:25], st_s[:, 8:9], 1.0 / DM))
        op("vector", lambda e: e.tensor_tensor(out=st_s[:, 32:33], in0=st_s[:, 16:17],
                                               in1=st_s[:, 16:17], op=ALU.mult))
        op("vector", spc)
        op("vector", spc)
        t_var = op("vector", lambda e: e.tensor_tensor(out=st_s[:, 40:41], in0=st_s[:, 24:25],
                                                       in1=st_s[:, 32:33], op=ALU.subtract), inc="VE")
        if dbg and t8 == 0:
            op("scalar", lambda e: e.activation(out=dbgep[:, 2:3], in_=epsT[:, :],
                                                func=AF.Copy), waits=[("VE", t_var)])
            op("scalar", lambda e: e.activation(out=dbgep[:, 3:4], in_=st_s[:, 16:17],
                                                func=AF.Copy))
        op("scalar", aspc, waits=[("VE", t_var)])
        op("scalar", aspc)
        t_sd_ = op("scalar", lambda e: e.activation(out=st_s[:, 48:49], in_=st_s[:, 40:41],
                                                    func=AF.Sqrt, bias=epsT[:, :]),
                   inc="AC")
        op("vector", spc, waits=[("AC", t_sd_)])
        op("vector", spc)
        op("vector", lambda e: e.reciprocal(st_s[:, 56:57], st_s[:, 48:49]))
        op("vector", spc)
        op("vector", spc)
        op("vector", lambda e: e.scalar_tensor_tensor(out=st_s[:, 32:33], in0=st_s[:, 16:17],
                                                      scalar=-1.0, in1=st_s[:, 56:57],
                                                      op0=ALU.mult, op1=ALU.mult))
        op("vector", spc)
        op("vector", spc)
        t_xnv = op("vector", lambda e, xb=xb, xn=xn: e.tensor_scalar(
            out=xn[:, :], in0=xb[:, :], scalar1=st_s[:, 56:57], scalar2=st_s[:, 32:33],
            op0=ALU.mult, op1=ALU.add), inc="VE",
            waits=[("PE", t_xnt_pe[t8 % 2])])
        t_stats_prev = t_xnv
        t_xn_cons[t8 % 2] = t_xnv
        if dbg:
            dma("scalar", dbg_xnt[128 * t8:128 * (t8 + 1), :], xn[:, :],
                waits=[("VE", t_xnv)])
            dma("scalar", dbg_sts[:, 64 * t8:64 * (t8 + 1)], st_s[:, :],
                waits=[("VE", t_xnv)])

        if t8 + 2 < 8:
            t_x.append(dma("sync", xbuf[t8 % 2][:, :], XR[t8 + 2], waits=[("VE", t_xnv)]))
        TRS = [(psA, 0), (psA, 512), (psB, 0), (psB, 512)]
        for g4 in range(8):
            for s2 in range(4):
                kc = 4 * g4 + s2
                bank, off = TRS[s2]
                w0 = []
                if s2 == 0:
                    w0 = [("VE", t_ev_last[0]), ("VE", t_xnv)]
                op("tensor", lambda e, xn=xn, kc=kc, bank=bank, off=off:
                   e.transpose(bank[:, off:off + 64].bitcast(BF16),
                               xn[:, 128 * kc:128 * (kc + 1)], ident[:, :]),
                   inc="PE" if s2 == 3 else None, waits=w0)
            t_tr = ctr["PE"]
            for s2 in range(4):
                kc = 4 * g4 + s2
                bank, off = TRS[s2]
                t_ev = op("vector", lambda e, bank=bank, off=off, kc=kc, t8=t8:
                          e.tensor_copy(
                              out=xn_fm[:, kc * TOWN + 128 * t8: kc * TOWN + 128 * (t8 + 1)],
                              in_=bank[:, off:off + 64].bitcast(BF16)),
                          inc="VE", waits=[("PE", t_tr)] if s2 == 0 else ())
            t_ev_last[0] = t_ev
        t_xnt_pe[t8 % 2] = ctr["PE"]
    if dbg:
        op("vector", lambda e: e.tensor_copy(out=dbgep[:, 0:1], in_=epsT[:, :]))
        op("vector", lambda e: e.tensor_copy(out=dbgep[:, 1:2], in_=negB[:, :]))
        t_dbgep = op("vector", lambda e: e.engine_nop(), inc="VE")
        dma("scalar", dbg_eps, dbgep[:, :], waits=[("VE", t_dbgep)])
    t_xn_done = op("vector", lambda e: e.engine_nop(), inc="VE")
    sfree("xbuf0", "xbuf1", "xnt0", "xnt1", "sqd", "st_s",
          *(["dbgep"] if dbg else []))

    if upto < 1:
        if dbg:
            wend("sync", dma_now("ST") + dma_now("LD") + [("VE", ctr["VE"])])
            dma("sync", dbg_xnfm, xn_fm[:, :])
            wend("sync", dma_now("LD"))
        return _finish(nc, prog, ctr, sems, wend, es, CCNAMES[:0])
    cdb = [sbuf(f"cdb{i}", [128, 32 * 128], I32) for i in range(2)]
    wst = [sbuf(f"wst{i}", [128, 32 * 128], BF16) for i in range(2)]
    qev = sbuf("qev", [128, 1024], BF16)
    rtmp1 = sbuf("rtmp1", [64, 512], F32)
    rtmp2 = sbuf("rtmp2", [64, 512], F32)
    kr1 = sbuf("kr1", [128, 64], F32)
    kr2 = sbuf("kr2", [128, 64], F32)
    kvst = sbuf("kvst", [128, 1024], BF16)
    pkb = [sbuf(f"pkb{i}", [128, 512], I32) for i in range(2)]
    ut1 = sbuf("ut1", [128, 512], I32)
    ut2f = [sbuf(f"ut2{i}", [128, 512], F32) for i in range(2)]
    ut3 = sbuf("ut3", [128, 512], F32)
    wdq = [sbuf(f"wdq{i}", [128, 512], BF16) for i in range(2)]
    gixb = sbuf("gixb", [1, 128], BF16)
    gixi = sbuf("gixi", [1, 128], I32)
    ohb = sbuf("ohb", [86, 128], BF16)

    t_cdb = [t_xn_done, t_xn_done]
    t_wst_cons = [0, 0]
    t_qev_cons = None
    t_kvst_cons = None
    t_ps23_cons = [t_ev_last[0], t_ev_last[0]]



    gptq_units = []

    def plan_gptq_shard(qw_dram, gi_dram, stab, ztab, G, rows, cols, out_dram, tsld, tzld):
        nrc = (rows + 127) // 128
        csl = []
        c0 = 0
        while c0 < cols:
            cw = min(512, cols - c0)
            csl.append((c0, cw))
            c0 += cw
        for rc in range(nrc):
            w = min(128, rows - 128 * rc)
            gptq_units.append(("oh", qw_dram, gi_dram, stab, ztab, G, rc, w, None,
                               out_dram, tsld, tzld))
            for cs in csl:
                gptq_units.append(("tile", qw_dram, gi_dram, stab, ztab, G, rc, w, cs,
                                   out_dram, tsld, tzld))

    plan_gptq_shard(GQW, GGI, sgt, zgt, 32, DM, ISL, wg_sh, t_sg, t_zgt)
    plan_gptq_shard(UQW, UGI, sut, zut, 32, DM, ISL, wu_sh, t_su, t_zut)
    plan_gptq_shard(DQW, DGI, sdt, zdt, 86, ISL, DM, wd_sh, t_sd, t_zdt)

    gp = {"i": 0, "par": 0, "t_oh": 0, "t_gix_cons": 0, "t_pk": [0, 0],
          "t_ut2_cons": [0, 0], "t_w_cons": [None, None], "t_chain": t_xn_done, "t_store": None,
          "t_ut1_cons": 0}

    def emit_gptq_some(n):
        for _ in range(n):
            if gp["i"] >= len(gptq_units):
                return
            kind, qw, gi, stab, ztab, G, rc, w, cs, odram, tsld, tzld = gptq_units[gp["i"]]
            gp["i"] += 1
            if kind == "oh":
                tld = dma("sync", gixi[0:1, 0:w],
                          gi.rearrange("(o n) -> o n", o=1)[:, 128 * rc:128 * rc + w],
                          waits=[("PE", gp["t_gix_cons"]), ("VE", t_xn_done)])
                t_cv = op("vector", lambda e, w=w: e.tensor_copy(
                    out=gixb[0:1, 0:w], in_=gixi[0:1, 0:w]),
                    inc="VE", waits=[tld, ("VE", gp["t_oh"])])
                t_bc = op("tensor", lambda e, w=w, G=G: e.matmul(
                    psA[0:G, 0:w], onesG[:, 0:G], gixb[0:1, 0:w], start=True, stop=True),
                    inc="PE", waits=[("VE", t_cv), ("VE", t_gp_init), ("VE", gp["t_chain"])])
                gp["t_gix_cons"] = t_bc
                gp["t_oh"] = op("vector", lambda e, G=G, w=w: e.tensor_scalar(
                    out=ohb[0:G, 0:w], in0=psA[0:G, 0:w],
                    scalar1=iota_c[0:G, :], scalar2=None, op0=ALU.is_equal),
                    inc="VE", waits=[("PE", t_bc)])
            else:
                c0, cw = cs
                par = gp["par"]
                gp["par"] ^= 1
                rpn = w // 8
                tpk = dma("sync", pkb[par][0:w, 0:cw],
                          qw[16 * rc:16 * rc + rpn, c0:c0 + cw]
                          .rearrange("r (o n) -> r o n", o=1).broadcast_to([rpn, 8, cw]),
                          waits=[("VE", gp["t_pk"][par]), ("VE", t_xn_done)])
                op("tensor", lambda e, G=G, w=w, stab=stab, c0=c0, cw=cw:
                   e.matmul(psA[0:w, 0:cw], ohb[0:G, 0:w], stab[0:G, c0:c0 + cw],
                            start=True, stop=True),
                   waits=[("VE", gp["t_oh"]), ("VE", gp["t_chain"]), tsld])
                t_g = op("tensor", lambda e, G=G, w=w, ztab=ztab, c0=c0, cw=cw:
                         e.matmul(psA[0:w, 512:512 + cw], ohb[0:G, 0:w], ztab[0:G, c0:c0 + cw],
                                  start=True, stop=True),
                         inc="PE", waits=[("VE", tzld)])
                t_u1 = op("vector", lambda e, par=par, w=w, cw=cw: e.tensor_scalar(
                    out=ut1[0:w, 0:cw].bitcast(U32INT), in0=pkb[par][0:w, 0:cw].bitcast(U32INT),
                    scalar1=nibm[0:w, :].bitcast(U32INT),
                    scalar2=None, op0=ALU.bitwise_and),
                    inc="VE",
                    waits=[tpk, t_consts])
                t_u2 = op("vector", lambda e, par=par, w=w, cw=cw: e.tensor_scalar(
                    out=ut2f[par][0:w, 0:cw], in0=ut1[0:w, 0:cw].bitcast(U32INT),
                    scalar1=nibp[0:w, :],
                    scalar2=None, op0=ALU.mult), inc="VE",
                    waits=[("VE", gp["t_ut2_cons"][par])])
                gp["t_ut1_cons"] = t_u2
                gp["t_pk"][par] = t_u1
                t_u3 = op("vector", lambda e, par=par, w=w, cw=cw: e.tensor_tensor(
                    out=ut3[0:w, 0:cw], in0=ut2f[par][0:w, 0:cw], in1=psA[0:w, 512:512 + cw],
                    op=ALU.subtract),
                    waits=[("PE", t_g), gp["t_w_cons"][par]])
                gp["t_ut2_cons"][par] = ctr["VE"] + 1
                gp["t_chain"] = op("vector", lambda e, par=par, w=w, cw=cw: e.tensor_tensor(
                    out=wdq[par][0:w, 0:cw], in0=ut3[0:w, 0:cw], in1=psA[0:w, 0:cw],
                    op=ALU.mult), inc="VE")
                gp["t_ut2_cons"][par] = gp["t_chain"]
                tst = dma("scalar", odram[128 * rc:128 * rc + w, c0:c0 + cw],
                          wdq[par][0:w, 0:cw], waits=[("VE", gp["t_chain"])])
                gp["t_w_cons"][par] = tst
                gp["t_store"] = tst

    fp4_prev_toks = [None]

    def emit_fp4_head(codes_dram, stab, col0, par, t_stab=0):
        tld = dma("sync", cdb[par][:, :],
                  codes_dram[:, col0:col0 + 128].rearrange("(kc p) n -> p kc n", p=128),
                  waits=[("VE", t_cdb[par])])
        toks = []
        prev = fp4_prev_toks[0]
        for kc in range(32):
            slot = 512 * (kc % 2)
            if kc >= 2:
                wg_ = [("VE", toks[kc - 2])]
            elif prev is not None:
                wg_ = [("VE", prev[kc + 30])]
            else:
                wg_ = [("VE", t_xn_done)]
            if kc == 0:
                wg_ = list(wg_) + [t_stab]
            t_g = op("tensor", lambda e, kc=kc, slot=slot, stab=stab, col0=col0:
                     e.matmul(psB[:, slot:slot + 128],
                              fp4oh[:, 128 * kc:128 * (kc + 1)], stab[:, col0:col0 + 128],
                              start=True, stop=True),
                     inc="PE", waits=wg_)
            wd_ = [("PE", t_g), tld]
            if kc == 0:
                wd_.append(("PE", t_wst_cons[par]))
                wd_.append(("VE", t_xn_done))
            t_dq = op("vector", lambda e, kc=kc, slot=slot, par=par:
                      e.scalar_tensor_tensor(
                          out=wst[par][:, 128 * kc:128 * (kc + 1)],
                          in0=cdb[par][:, 128 * kc:128 * (kc + 1)], scalar=8.0,
                          in1=psB[:, slot:slot + 128],
                          op0=ALU.subtract, op1=ALU.mult),
                      inc="VE", waits=wd_)
            toks.append(t_dq)
        t_cdb[par] = toks[-1]
        fp4_prev_toks[0] = toks
        return toks

    import os as _os
    _hsel = _os.environ.get("KHEADS", "all")
    nheads = [("q", h) for h in range(H)] + [("k", j) for j in range(KVH)] + \
             [("v", j) for j in range(KVH)]
    if _hsel == "q":
        nheads = nheads[:H]
    elif _hsel == "kv":
        nheads = nheads[H:]
    elif _hsel == "q1":
        nheads = nheads[:1]
    elif _hsel == "kv1":
        nheads = nheads[H:H + 1]
    t_kin_st = 0
    t_vin_st = 0
    t_q_st = []
    par = 0
    for (kind, h) in nheads:
        codes = {"q": QC, "k": KC, "v": VC}[kind]
        dq_toks = emit_fp4_head(codes, s4[kind], 128 * h, par, t_s4tok[kind])
        if kind == "q":
            t_mm = {}
            for half in range(2):
                bank = [ps2, ps3][half]
                for kc in range(32):
                    wts = [("VE", dq_toks[min(kc + 2, 31)])]
                    if kc == 0:
                        wts += [("VE", t_ps23_cons[half]), ("VE", t_xn_done)]
                    t_mm[half] = op("tensor", lambda e, kc=kc, bank=bank, half=half, par=par:
                                    e.matmul(bank[:, :],
                                             wst[par][:, 128 * kc:128 * (kc + 1)],
                                             xn_fm[:, kc * TOWN + 512 * half:kc * TOWN + 512 * (half + 1)],
                                             start=(kc == 0), stop=(kc == 31)),
                                    inc="PE" if kc == 31 else None, waits=wts)
            t_wst_cons[par] = t_mm[1]
            t_r = 0
            for half in range(2):
                bank = [ps2, ps3][half]
                cqs = rqc[:, 512 * half:512 * (half + 1)]
                sns = rqs[:, 512 * half:512 * (half + 1)]
                op("vector", lambda e, bank=bank, cqs=cqs: e.tensor_tensor(
                    out=rtmp1[:, :], in0=bank[0:64, :], in1=cqs, op=ALU.mult),
                    waits=[("PE", t_mm[half]), t_qev_cons])
                op("vector", lambda e, bank=bank, sns=sns: e.tensor_tensor(
                    out=rtmp2[:, :], in0=bank[64:128, :], in1=sns, op=ALU.mult))
                op("vector", lambda e, half=half: e.tensor_tensor(
                    out=qev[0:64, 512 * half:512 * (half + 1)], in0=rtmp1[:, :],
                    in1=rtmp2[:, :], op=ALU.subtract))
                op("vector", lambda e, bank=bank, cqs=cqs: e.tensor_tensor(
                    out=rtmp1[:, :], in0=bank[64:128, :], in1=cqs, op=ALU.mult))
                op("vector", lambda e, bank=bank, sns=sns: e.tensor_tensor(
                    out=rtmp2[:, :], in0=bank[0:64, :], in1=sns, op=ALU.mult))
                t_r = op("vector", lambda e, half=half: e.tensor_tensor(
                    out=qev[64:128, 512 * half:512 * (half + 1)], in0=rtmp1[:, :],
                    in1=rtmp2[:, :], op=ALU.add), inc="VE")
                t_ps23_cons[half] = t_r
            t_qev_cons = dma("scalar", q_store[128 * h:128 * (h + 1), :], qev[:, :],
                             waits=[("VE", t_r)])
            t_q_st.append(t_qev_cons)
        else:
            for tt in range(8):
                bank = [ps2, ps3][tt % 2]
                for kc in range(32):
                    wts = [("VE", dq_toks[min(kc + 2, 31)])]
                    if kc == 0:
                        wts += [("VE", t_ps23_cons[tt % 2]), ("VE", t_xn_done)]
                    tk = op("tensor", lambda e, kc=kc, tt=tt, bank=bank, par=par:
                            e.matmul(bank[:, 0:128],
                                     xn_fm[:, kc * TOWN + 128 * tt:kc * TOWN + 128 * (tt + 1)],
                                     wst[par][:, 128 * kc:128 * (kc + 1)],
                                     start=(kc == 0), stop=(kc == 31)),
                            inc="PE" if kc == 31 else None, waits=wts)
                cqs = rkc[:, 64 * tt:64 * (tt + 1)]
                sns = rks[:, 64 * tt:64 * (tt + 1)]
                if kind == "k":
                    op("vector", lambda e, bank=bank, cqs=cqs: e.tensor_tensor(
                        out=kr1[:, :], in0=bank[:, 0:64], in1=cqs, op=ALU.mult),
                        waits=[("PE", tk), t_kvst_cons])
                    op("vector", lambda e, bank=bank, sns=sns: e.tensor_tensor(
                        out=kr2[:, :], in0=bank[:, 64:128], in1=sns, op=ALU.mult))
                    op("vector", lambda e: e.tensor_scalar_mul(junkc[:, :], iota_c[:, :], 1.0))
                    op("vector", lambda e, tt=tt: e.tensor_tensor(
                        out=kvst[:, 128 * tt:128 * tt + 64], in0=kr1[:, :],
                        in1=kr2[:, :], op=ALU.subtract))
                    op("vector", lambda e, bank=bank, cqs=cqs: e.tensor_tensor(
                        out=kr1[:, :], in0=bank[:, 64:128], in1=cqs, op=ALU.mult))
                    op("vector", lambda e, bank=bank, sns=sns: e.tensor_tensor(
                        out=kr2[:, :], in0=bank[:, 0:64], in1=sns, op=ALU.mult))
                    op("vector", lambda e: e.tensor_scalar_mul(junkc[:, :], iota_c[:, :], 1.0))
                    t_e = op("vector", lambda e, tt=tt: e.tensor_tensor(
                        out=kvst[:, 128 * tt + 64:128 * (tt + 1)], in0=kr1[:, :],
                        in1=kr2[:, :], op=ALU.add), inc="VE")
                else:
                    t_e = op("vector", lambda e, bank=bank, tt=tt: e.tensor_copy(
                        out=kvst[:, 128 * tt:128 * (tt + 1)], in_=bank[:, 0:128]),
                        inc="VE", waits=[("PE", tk), t_kvst_cons])
                t_ps23_cons[tt % 2] = t_e
            t_wst_cons[par] = tk
            op("vector", lambda e: e.tensor_scalar_mul(junkc[:, :], iota_c[:, :], 1.0))
            op("vector", lambda e: e.tensor_scalar_mul(junkc[:, :], iota_c[:, :], 1.0))
            t_kv_settle = op("vector", lambda e: e.engine_nop(), inc="VE")
            dest = kin if kind == "k" else vin
            for tt in range(8):
                t_kvst_cons = dma("scalar",
                                  dest[128 * tt:128 * (tt + 1), 128 * h:128 * (h + 1)],
                                  kvst[:, 128 * tt:128 * (tt + 1)],
                                  waits=([("VE", t_kv_settle)] if tt == 0 else ()))
            if kind == "k":
                t_kin_st = t_kvst_cons
            else:
                t_vin_st = t_kvst_cons
        par ^= 1
        if not nogptq:
            emit_gptq_some(8)
    if not nogptq:
        emit_gptq_some(len(gptq_units))

    RG = [list(range(NC))]

    def coll(sem, in_ap, out_ap, waits):
        op("gpsimd", lambda e, i=in_ap, o=out_ap: e.collective_compute(
            "AllGather", ALU.bypass, replica_groups=RG, ins=[i.opt()], outs=[o.opt()]),
            inc=sem, waits=waits)

    if not nocoll:
        coll("cK0", kin[0:THALF, :], kout[0], dma_now("ST"))
        coll("cV0", vin[0:THALF, :], vout[0], [])
        coll("cK1", kin[THALF:TOWN, :], kout[1], [])
        coll("cV1", vin[THALF:TOWN, :], vout[1], [])
        if not nogptq:
            coll("cWG", wg_sh[:, :], wgF, dma_now("ST"))
            coll("cWU", wu_sh[:, :], wuF, [])
            coll("cWD", wd_sh[:, :], wdF, [])

    sfree("sgt", "sut", "sdt", "zgt", "zut", "zdt", "zraw",
          "pkb0", "pkb1", "ut1", "ut20", "ut21", "ut3", "wdq0", "wdq1",
          "cdb0", "cdb1", "qev", "kvst", "gixb", "gixi", "ohb",
          "rtmp1", "rtmp2", "kr1", "kr2", "wst0", "wst1")

    if upto < 2:
        if dbg:
            wts = dma_now("ST") + dma_now("LD") + [("VE", ctr["VE"])] +                   ([] if nocoll else [(s, 1) for s in
                                      (CCNAMES[:4] if nogptq else CCNAMES)])
            wend("sync", wts)
            for o, srcap in dbg_outs:
                dma("sync", o, srcap)
            if not nogptq:
                dma("sync", dbg_wgf, wgF[0:DM, :])
            dma("sync", dbg_xnfm, xn_fm[:, :])
            wend("sync", dma_now("LD"))
        return _finish(nc, prog, ctr, sems, wend, es,
                       [] if nocoll else (CCNAMES[:4] if nogptq else CCNAMES))
    barrier()
    dmask = sbuf("dmask", [128, 8 * 512], BF16)
    t_dmask = dma("sync", dmask[:, :], DMASK)
    kfm = sbuf("kfm", [128, 64 * 128], BF16)
    vaug = sbuf("vaug", [128, 64 * 132], BF16)
    qbuf = [sbuf(f"qbuf{i}", [128, 8 * 512], BF16) for i in range(2)]
    pbuf = [sbuf(f"pbuf{i}", [128, 1024], BF16) for i in range(2)]
    ktmp = [sbuf(f"ktmp{i}", [128, 128], BF16) for i in range(2)]
    aofm4 = [sbuf(f"aofm4{i}", [128, 512], BF16) for i in range(2)]
    nrec4 = [sbuf(f"nrec4{i}", [128, 4], F32) for i in range(2)]
    ACC = [ps2, ps3, ps4, ps5]

    op("vector", lambda e: e.memset(vaug[:, :], 0.0))
    for kt in range(64):
        op("vector", lambda e, kt=kt: e.memset(vaug[:, 132 * kt + 128:132 * kt + 129], 4.0))
    t_vones = op("vector", lambda e: e.engine_nop(), inc="VE")

    t_kfm_cons = 0
    t_vaug_cons = 0
    t_q_cons = [0, 0]
    t_pb_cons = [0, 0]
    t_sc_cons = [0, 0]
    t_acc_ev4 = [0, 0, 0, 0]
    t_tr_ev = [0, 0]
    t_ktmp_cons = [0, 0]
    t_ao_dma = [None, None]

    for j in range(KVH):
        t_kfm_prev = t_kfm_cons
        t_vaug_prev = t_vaug_cons
        qp = j % 2
        for kt in range(64):
            r = kt // 32
            ktl = kt - 32 * r
            src_ = kout[r][:, 128 * j:128 * (j + 1)] \
                .rearrange("(s i) d -> s i d", s=NC)[:, 16 * ktl:16 * ktl + 16, :]
            tld = dma("sync", ktmp[kt % 2][:, :], src_,
                      waits=[("PE", t_ktmp_cons[kt % 2]), ("cK%d" % r, 1)])
            bank = [psA, psB][kt % 2]
            t_tr = op("tensor", lambda e, kt=kt, bank=bank: e.transpose(
                bank[:, 0:64].bitcast(BF16), ktmp[kt % 2][:, :], ident[:, :]),
                inc="PE",
                waits=[tld, ("VE", t_tr_ev[kt % 2])] +
                      ([("AC", t_sc_cons[kt % 2])] if kt < 2 else []))
            t_ktmp_cons[kt % 2] = t_tr
            t_ev = op("vector", lambda e, kt=kt, bank=bank: e.tensor_copy(
                out=kfm[:, 128 * kt:128 * (kt + 1)], in_=bank[:, 0:64].bitcast(BF16)),
                inc="VE", waits=[("PE", t_tr), ("PE", t_kfm_prev)])
            t_tr_ev[kt % 2] = t_ev
            vsrc = vout[r][:, 128 * j:128 * (j + 1)] \
                .rearrange("(s i) d -> s i d", s=NC)[:, 16 * ktl:16 * ktl + 16, :]
            dma("sync", vaug[:, 132 * kt:132 * kt + 128], vsrc,
                waits=[("PE", t_vaug_prev), ("VE", t_vones), ("cV%d" % r, 1)])
        t_kv_ready = op("vector", lambda e: e.engine_nop(), inc="VE")
        t_vld = dma_now("LD")
        t_ql = None
        for h4 in range(4):
            hq = 4 * j + h4
            t_ql = dma("sync",
                       qbuf[qp][:, :].rearrange("p (t h q) -> p t h q", h=4, q=128)[:, :, h4, :],
                       q_store[128 * hq:128 * (hq + 1), :]
                       .rearrange("p (t q) -> p t q", q=128),
                       waits=[("PE", t_q_cons[qp]), t_q_st[hq]] if h4 == 0 else
                             [t_q_st[hq]])

        pend = [None]

        def flush_pend():
            if pend[0] is None:
                return None
            pp, p8, t, wtok, is_ac = pend[0]
            pend[0] = None
            t_pv = None
            npair = 4 * (t + 1)
            for ktp in range(2):
                kt = 2 * p8 + ktp
                for h4 in range(4):
                    wpv = []
                    if ktp == 0 and h4 == 0:
                        wpv = [(("AC" if is_ac else "VE"), wtok)]
                    if p8 == 0 and ktp == 0:
                        wpv.append(("AC", t_acc_ev4[h4]))
                    t_pv = op("tensor", lambda e, kt=kt, h4=h4, pp=pp, p8=p8, t=t, ktp=ktp, npair=npair:
                              e.matmul(ACC[h4][:, 0:132],
                                       pbuf[pp][:, 512 * ktp + 128 * h4:512 * ktp + 128 * (h4 + 1)],
                                       vaug[:, 132 * kt:132 * (kt + 1)],
                                       start=(p8 == 0 and ktp == 0),
                                       stop=(p8 == npair - 1 and ktp == 1),
                                       skip_group_check=True),
                              inc="PE" if (ktp == 1 and h4 == 3) else None, waits=wpv)
            t_pb_cons[pp] = t_pv
            return t_pv

        for t in range(8):
            npair = 4 * (t + 1)
            for p8 in range(npair):
                pp = p8 % 2
                scb = [psA, psB][pp]
                t_sc = None
                for ktp in range(2):
                    kt = 2 * p8 + ktp
                    wts = []
                    if ktp == 0:
                        wts = [("AC", t_sc_cons[pp]), t_ql,
                               ("VE", t_kv_ready)] + t_vld
                    t_sc = op("tensor", lambda e, kt=kt, ktp=ktp, scb=scb, qp=qp, t=t:
                              e.matmul(scb[:, 512 * ktp:512 * (ktp + 1)],
                                       kfm[:, 128 * kt:128 * (kt + 1)],
                                       qbuf[qp][:, 512 * t:512 * (t + 1)],
                                       start=True, stop=True),
                              inc="PE" if ktp == 1 else None, waits=wts)
                t_kfm_cons = t_sc
                t_vaug_cons_prev = flush_pend()
                if t_vaug_cons_prev:
                    t_vaug_cons = t_vaug_cons_prev
                t_exp = op("scalar", lambda e, scb=scb, pp=pp: e.activation(
                    out=pbuf[pp][:, :], in_=scb[:, :], func=AF.Exp, bias=negB[:, :]),
                    inc="AC", waits=[("PE", t_sc), ("PE", t_pb_cons[pp]),
                                     ("VE", t_gp_init)])
                t_sc_cons[pp] = t_exp
                if p8 >= 4 * t:
                    d0 = 2 * (p8 - 4 * t)
                    op("vector", lambda e, pp=pp, d0=d0: e.tensor_tensor(
                        out=pbuf[pp][:, 0:512], in0=pbuf[pp][:, 0:512],
                        in1=dmask[:, 512 * d0:512 * (d0 + 1)], op=ALU.mult),
                        waits=[("AC", t_exp), t_dmask])
                    t_mask = op("vector", lambda e, pp=pp, d0=d0: e.tensor_tensor(
                        out=pbuf[pp][:, 512:1024], in0=pbuf[pp][:, 512:1024],
                        in1=dmask[:, 512 * (d0 + 1):512 * (d0 + 2)], op=ALU.mult),
                        inc="VE")
                    pend[0] = (pp, p8, t, t_mask, False)
                else:
                    pend[0] = (pp, p8, t, t_exp, True)
            t_pv = flush_pend()
            t_vaug_cons = t_pv
            ppar = t % 2
            op("vector", lambda e: e.tensor_scalar_mul(junkc[:, :], iota_c[:, :], 1.0),
               waits=[("PE", t_pv), ("AC", t_acc_ev4[0]), ("AC", t_acc_ev4[1]),
                      ("AC", t_acc_ev4[2]), ("AC", t_acc_ev4[3])])
            op("vector", lambda e: e.tensor_scalar_mul(junkc[:, :], iota_c[:, :], 1.0))
            t_rc = None
            for h4 in range(4):
                t_rc = op("vector", lambda e, h4=h4, ppar=ppar: e.reciprocal(
                    nrec4[ppar][:, h4:h4 + 1], ACC[h4][:, 128:129]),
                    inc="VE" if h4 == 3 else None)
            for h4 in range(4):
                t_ae = op("scalar", lambda e, h4=h4, ppar=ppar: e.activation(
                    out=aofm4[ppar][:, 128 * h4:128 * (h4 + 1)], in_=ACC[h4][:, 0:128],
                    func=AF.Copy, scale=nrec4[ppar][:, h4:h4 + 1]),
                    inc="AC",
                    waits=[("VE", t_rc)] + ([t_ao_dma[ppar]] if h4 == 0 else []))
                t_acc_ev4[h4] = t_ae
            t_ao_dma[ppar] = dma("scalar",
                                 ao_store[128 * t:128 * (t + 1), 512 * j:512 * (j + 1)],
                                 aofm4[ppar][:, :], waits=[("AC", t_ae)])
        t_q_cons[qp] = t_pv
    t_ao_done = [t for t in t_ao_dma if t]

    sfree("kfm", "vaug", "qbuf0", "qbuf1", "ktmp0", "ktmp1",
          "aofm40", "aofm41", "nrec40", "nrec41", "pbuf0", "pbuf1")

    if upto < 3:
        return _finish(nc, prog, ctr, sems, wend, es, CCNAMES)
    barrier()
    wosl = sbuf("wosl", [128, 32 * 512], BF16)
    ocb = [sbuf(f"ocb{i}", [128, 512], I32) for i in range(2)]
    aob = [sbuf(f"aob{i}", [128, 512], BF16) for i in range(2)]
    sev = [sbuf(f"sev{i}", [128, 512], BF16) for i in range(2)]
    t_oc_cons = [0, 0]
    t_aob_cons = [0, 0]
    t_wo_cons = 0
    t_sev_dma = [None, None]
    t_wv_ev = [0, 0, 0, 0]

    dq3_prev = None
    for g8 in range(8):
        dq3 = []
        for kc in range(32):
            pr = kc % 2
            tld = dma("sync", ocb[pr][:, :], OC[128 * kc:128 * (kc + 1), 512 * g8:512 * (g8 + 1)],
                      waits=[("VE", t_oc_cons[pr])])
            if kc >= 2:
                wg_ = [("VE", dq3[kc - 2])]
            elif dq3_prev is not None:
                wg_ = [("VE", dq3_prev[kc + 30])]
            else:
                wg_ = [t_s4tok["o"]]
            t_g = op("tensor", lambda e, kc=kc, pr=pr, g8=g8: e.matmul(
                psA[:, 512 * pr:512 * (pr + 1)], fp4oh[:, 128 * kc:128 * (kc + 1)],
                s4["o"][:, 512 * g8:512 * (g8 + 1)], start=True, stop=True),
                inc="PE", waits=wg_)
            t_dq = op("vector", lambda e, kc=kc, pr=pr:
                      e.scalar_tensor_tensor(out=wosl[:, 512 * kc:512 * (kc + 1)],
                                             in0=ocb[pr][:, :], scalar=8.0,
                                             in1=psA[:, 512 * pr:512 * (pr + 1)],
                                             op0=ALU.subtract, op1=ALU.mult),
                      inc="VE", waits=[("PE", t_g), tld])
            t_oc_cons[pr] = t_dq
            dq3.append(t_dq)
        dq3_prev = dq3
        for wave in range(2):
            t_mm = None
            for kc in range(32):
                pr = kc % 2
                tld = dma_op("sync", lambda e, pr=pr, kc=kc, wave=wave: e.dma_start_transpose(
                    aob[pr][:, :],
                    ao_store[512 * wave:512 * (wave + 1), 128 * kc:128 * (kc + 1)]),
                    waits=[("PE", t_aob_cons[pr])] + t_ao_done)
                for tt in range(4):
                    wts = []
                    if tt == 0:
                        wts = [tld, ("VE", dq3[kc])]
                    if kc == 0:
                        wts.append(("AC", t_wv_ev[tt]))
                    t_mm = op("tensor", lambda e, kc=kc, tt=tt, pr=pr:
                              e.matmul(PSW[tt][:, :],
                                       aob[pr][:, 128 * tt:128 * (tt + 1)],
                                       wosl[:, 512 * kc:512 * (kc + 1)],
                                       start=(kc == 0), stop=(kc == 31)),
                              inc="PE" if tt == 3 else None, waits=wts)
                t_aob_cons[pr] = t_mm
            t_wo_cons = t_mm
            for tt in range(4):
                pr2 = tt % 2
                t_e = op("scalar", lambda e, tt=tt, pr2=pr2: e.activation(
                    out=sev[pr2][:, :], in_=PSW[tt][:, :], func=AF.Copy),
                    inc="AC", waits=[("PE", t_mm), t_sev_dma[pr2]])
                t_wv_ev[tt] = t_e
                t_sev_dma[pr2] = dma(
                    "scalar",
                    s_store[:, 512 * g8:512 * (g8 + 1)]
                    .rearrange("(w tt p) n -> w tt p n", w=2, p=128)[wave, tt],
                    sev[pr2][:, :], waits=[("AC", t_e)])
    t_s_done = [t for t in t_sev_dma if t]

    sfree("wosl", "ocb0", "ocb1", "aob0", "aob1", "sev0", "sev1")

    if upto < 4:
        return _finish(nc, prog, ctr, sems, wend, es, CCNAMES)
    barrier()
    sfree("s4q", "s4k", "s4v", "s4o", "dmask", "rqc", "rqs", "rkc", "rks",
          "ident", "iota_c", "junkc", "fp4oh_t", "negB", "onesG", "epsT",
          "nibm_t", "nibp_t")
    hhalf = sbuf("hhalf", [128, 88 * 512], BF16)
    wgb = [sbuf(f"wgb{i}", [128, 32 * 128], BF16) for i in range(2)]
    wub = [sbuf(f"wub{i}", [128, 32 * 128], BF16) for i in range(2)]
    hg = sbuf("hg", [128, 512], BF16)
    wdb = [sbuf(f"wdb{i}", [128, 512], BF16) for i in range(4)]
    gts = sbuf("gts", [128, 512], F32)
    xpb = [sbuf(f"xpb{i}", [128, 512], F32) for i in range(2)]
    spb = [sbuf(f"spb{i}", [128, 512], BF16) for i in range(2)]
    oub = [sbuf(f"oub{i}", [128, 512], F32) for i in range(2)]
    t_wg_cons = [0, 0]
    t_wu_cons = [0, 0]
    t_h_cons = 0
    t_wd_cons = [0, 0, 0, 0]
    t_xp_cons = [0, 0]
    t_sp_cons = [0, 0]
    t_ou_dma = [None, None]
    t_gu_ev = [(0, 0), (0, 0)]
    t_gts_cons = 0
    t_down_ev = [0, 0, 0, 0]

    for hh in range(2):
        t_hm = 0
        for s8 in range(8):
            for ic in range(NIC):
                w = IW[ic]
                u = s8 * NIC + ic
                pr = u % 2
                gb = [psA, psB][pr]
                tldg = dma("sync", wgb[pr][:, 0:32 * w],
                           wgF[DM * s8:DM * (s8 + 1), IOFF[ic]:IOFF[ic] + w]
                           .rearrange("(kc p) i -> p kc i", p=128),
                           waits=[("PE", t_wg_cons[pr]), ("cWG", 1)])
                tldu = dma("sync", wub[pr][:, 0:32 * w],
                           wuF[DM * s8:DM * (s8 + 1), IOFF[ic]:IOFF[ic] + w]
                           .rearrange("(kc p) i -> p kc i", p=128),
                           waits=[("PE", t_wu_cons[pr]), ("cWU", 1)])
                t_mg = t_mu = None
                for kc in range(32):
                    wts = [tldg, ("AC", t_gu_ev[pr][0]),
                           ("VE", t_gu_ev[pr][1])] if kc == 0 else []
                    t_mg = op("tensor", lambda e, kc=kc, w=w, pr=pr, gb=gb, hh=hh:
                              e.matmul(gb[0:w, 0:512],
                                       wgb[pr][:, w * kc:w * (kc + 1)],
                                       xn_fm[:, kc * TOWN + 512 * hh:kc * TOWN + 512 * (hh + 1)],
                                       start=(kc == 0), stop=(kc == 31)),
                              inc="PE" if kc == 31 else None, waits=wts)
                for kc in range(32):
                    wts = [tldu, ("VE", t_gu_ev[pr][1])] if kc == 0 else []
                    t_mu = op("tensor", lambda e, kc=kc, w=w, pr=pr, gb=gb, hh=hh:
                              e.matmul(gb[0:w, 512:1024],
                                       wub[pr][:, w * kc:w * (kc + 1)],
                                       xn_fm[:, kc * TOWN + 512 * hh:kc * TOWN + 512 * (hh + 1)],
                                       start=(kc == 0), stop=(kc == 31)),
                              inc="PE" if kc == 31 else None, waits=wts)
                t_wg_cons[pr] = t_mg
                t_wu_cons[pr] = t_mu
                t_si = op("scalar", lambda e, w=w, gb=gb: e.activation(
                    out=hg[0:w, :], in_=gb[0:w, 0:512], func=AF.Silu),
                    inc="AC", waits=[("PE", t_mg), ("VE", t_hm)])
                wts = [("AC", t_si), ("PE", t_mu)]
                if u == 0:
                    wts.append(("PE", t_h_cons))
                t_hm = op("vector", lambda e, w=w, u=u, gb=gb: e.tensor_tensor(
                    out=hhalf[0:w, 512 * u:512 * (u + 1)], in0=hg[0:w, :],
                    in1=gb[0:w, 512:1024], op=ALU.mult), inc="VE", waits=wts)
                t_gu_ev[pr] = (t_si, t_hm)
        t_h_done = t_hm
        for g8 in range(8):
            first = True
            t_dn = None
            for s8 in range(8):
                for ic in range(NIC):
                    w = IW[ic]
                    u = s8 * NIC + ic
                    pr4 = u % 4
                    r0 = ISL * s8 + IOFF[ic]
                    tld = dma("sync", wdb[pr4][0:w, :],
                              wdF[r0:r0 + w, 512 * g8:512 * (g8 + 1)],
                              waits=[("PE", t_wd_cons[pr4]), ("cWD", 1)])
                    last = (s8 == 7 and ic == NIC - 1)
                    for tt in range(4):
                        wts = []
                        if tt == 0:
                            wts = [tld, ("VE", t_h_done)]
                        if first:
                            wts.append(("AC", t_down_ev[tt]))
                        t_dn = op("tensor",
                                  lambda e, w=w, u=u, tt=tt, pr4=pr4, f=first, l=last:
                                  e.matmul(PSW[tt][:, :],
                                           hhalf[0:w, 512 * u + 128 * tt:512 * u + 128 * (tt + 1)],
                                           wdb[pr4][0:w, :], start=f, stop=l),
                                  inc="PE" if tt == 3 else None, waits=wts)
                    t_wd_cons[pr4] = t_dn
                    first = False
            t_h_cons = t_dn
            for tt in range(4):
                pr = tt % 2
                t_gt = op("scalar", lambda e, tt=tt: e.activation(
                    out=gts[:, :], in_=PSW[tt][:, :], func=AF.Silu),
                    inc="AC", waits=[("PE", t_dn), ("VE", t_gts_cons)])
                t_down_ev[tt] = t_gt
                rows0 = 512 * hh + 128 * tt
                tldx = dma("sync", xpb[pr][:, :], X[rows0:rows0 + 128, 512 * g8:512 * (g8 + 1)],
                           waits=[("VE", t_xp_cons[pr])])
                tlds = dma("sync", spb[pr][:, :],
                           s_store[rows0:rows0 + 128, 512 * g8:512 * (g8 + 1)],
                           waits=[("VE", t_sp_cons[pr])] + t_s_done)
                op("vector", lambda e, pr=pr: e.tensor_tensor(
                    out=gts[:, :], in0=spb[pr][:, :], in1=gts[:, :], op=ALU.mult),
                    waits=[("AC", t_gt), tlds, t_ou_dma[pr]])
                t_ou = op("vector", lambda e, pr=pr: e.tensor_tensor(
                    out=oub[pr][:, :], in0=gts[:, :], in1=xpb[pr][:, :], op=ALU.add),
                    inc="VE", waits=[tldx])
                t_gts_cons = t_ou
                t_xp_cons[pr] = t_ou
                t_sp_cons[pr] = t_ou
                t_ou_dma[pr] = dma("scalar", OUT[rows0:rows0 + 128, 512 * g8:512 * (g8 + 1)],
                                   oub[pr][:, :], waits=[("VE", t_ou)])

    if dbg:
        wend("sync", dma_now("ST") + dma_now("LD") +
             ([] if nocoll else [(n, 1) for n in (CCNAMES[:4] if nogptq else CCNAMES)]))
        for o, srcap in dbg_outs:
            dma("sync", o, srcap)
        dma("sync", dbg_wgf, wgF[0:DM, :])
        dma("sync", dbg_xnfm, xn_fm[:, :])
        wend("sync", dma_now("LD"))
    return _finish(nc, prog, ctr, sems, wend, es,
                   [] if nocoll else (CCNAMES[:4] if nogptq else CCNAMES))


def _finish(nc, prog, ctr, sems, wend, es, ccnames):
    nc._dbg_ctr = dict(ctr)
    wend("scalar", [(s, v) for s, v in ctr.items() if s.startswith("ST") and v > 0])
    wend("sync", [(s, v) for s, v in ctr.items() if s.startswith("LD") and v > 0])
    wend("gpsimd", [(s, v) for s, v in ctr.items() if s.startswith("GP") and v > 0] +
         [(n, 1) for n in ccnames])

    with nc.Block() as block:
        @block.sync
        def _(e):
            for f in prog["sync"]:
                f(e)

        @block.gpsimd
        def _(e):
            for f in prog["gpsimd"]:
                f(e)

        @block.vector
        def _(e):
            for f in prog["vector"]:
                f(e)

        @block.scalar
        def _(e):
            for f in prog["scalar"]:
                f(e)

        @block.tensor
        def _(e):
            for f in prog["tensor"]:
                f(e)

    es.close()
    return nc


def _in_maps(inputs):
    x = np.asarray(inputs["L0_attn_x"])
    maps = []
    for c in range(NC):
        m = {
            "x_own": np.ascontiguousarray(x[0, c::NC, :], dtype=np.float32),
            "q_codes": inputs["L0_attn_q_codes"],
            "k_codes": inputs["L0_attn_k_codes"],
            "v_codes": inputs["L0_attn_v_codes"],
            "o_codes": inputs["L0_attn_o_codes"],
            "q_scales": inputs["L0_attn_q_scales"],
            "k_scales": inputs["L0_attn_k_scales"],
            "v_scales": inputs["L0_attn_v_scales"],
            "o_scales": inputs["L0_attn_o_scales"],
            "g_qw": np.ascontiguousarray(inputs["L0_ffn_gate_qweight"][:, ISL * c:ISL * (c + 1)]),
            "g_sc": np.ascontiguousarray(inputs["L0_ffn_gate_scales"][:, ISL * c:ISL * (c + 1)]),
            "g_qz": np.ascontiguousarray(inputs["L0_ffn_gate_qzeros"][:, (ISL // 8) * c:(ISL // 8) * (c + 1)]),
            "g_gi": inputs["L0_ffn_gate_g_idx"],
            "u_qw": np.ascontiguousarray(inputs["L0_ffn_up_qweight"][:, ISL * c:ISL * (c + 1)]),
            "u_sc": np.ascontiguousarray(inputs["L0_ffn_up_scales"][:, ISL * c:ISL * (c + 1)]),
            "u_qz": np.ascontiguousarray(inputs["L0_ffn_up_qzeros"][:, (ISL // 8) * c:(ISL // 8) * (c + 1)]),
            "u_gi": inputs["L0_ffn_up_g_idx"],
            "d_qw": np.ascontiguousarray(inputs["L0_ffn_down_qweight"][(ISL // 8) * c:(ISL // 8) * (c + 1), :]),
            "d_sc": inputs["L0_ffn_down_scales"],
            "d_qz": inputs["L0_ffn_down_qzeros"],
            "d_gi": np.ascontiguousarray(inputs["L0_ffn_down_g_idx"][ISL * c:ISL * (c + 1)]),
        }
        m.update(_host_consts(c))
        maps.append({k: np.ascontiguousarray(v) for k, v in m.items()})
    return maps


def kernel(**inputs):
    import os
    nc = build(int(os.environ.get("KUPTO", "99")),
               nogptq=bool(int(os.environ.get("KNOGPTQ", "0"))),
               nocoll=bool(int(os.environ.get("KNOCOLL", "0"))),
               dbg=bool(int(os.environ.get("KDBG", "0"))))
    if int(os.environ.get("KDBG", "0")):
        maps = _in_maps(inputs)
        res = run_bass_kernel_spmd(nc, maps, core_ids=list(range(NC)))
        globals()["_dbg_results"] = res.results[0]
        out = np.empty((1, S, DM), dtype=np.float32)
        for c in range(NC):
            out[0, c::NC, :] = res.results[c]["out_own"]
        return out
    maps = _in_maps(inputs)
    res = run_bass_kernel_spmd(nc, maps, core_ids=list(range(NC)))
    out = np.empty((1, S, DM), dtype=np.float32)
    for c in range(NC):
        out[0, c::NC, :] = res.results[c]["out_own"]
    return out


# version 09
# speedup vs baseline: 1.0249x; 1.0249x over previous
import math
import numpy as np
import ml_dtypes
from contextlib import ExitStack

import concourse.bass as bass
import concourse.mybir as mybir
from concourse.bass_utils import run_bass_kernel_spmd

F32 = mybir.dt.float32
BF16 = mybir.dt.bfloat16
I32 = mybir.dt.int32
U32INT = mybir.dt.uint32
AF = mybir.ActivationFunctionType
ALU = mybir.AluOpType
AXX = mybir.AxisListType.X

NC = 8
S = 8192
DM = 4096
H = 32
KVH = 8
D = 128
HD2 = 64
INTER = 11008
ISL = INTER // NC
TOWN = S // NC
THALF = TOWN // 2
B_EXP = 25.0
THETA = 10000.0

IW = [128] * 10 + [96]
IOFF = [sum(IW[:i]) for i in range(len(IW))]
NIC = len(IW)


def _bf(x):
    return np.ascontiguousarray(np.asarray(x).astype(ml_dtypes.bfloat16))


def _host_consts(c):
    own_abs = 8 * np.arange(TOWN) + c
    inv = 1.0 / THETA ** (np.arange(HD2, dtype=np.float64) / HD2)
    ang = own_abs[:, None].astype(np.float64) * inv[None, :]
    fq = 1.0 / (4.0 * math.sqrt(D))
    out = {
        "ropeq_cos": np.ascontiguousarray((np.cos(ang).T * fq).astype(np.float32)),
        "ropeq_sin": np.ascontiguousarray((np.sin(ang).T * fq).astype(np.float32)),
        "ropek_cos_t": np.ascontiguousarray(np.cos(ang).astype(np.float32)),
        "ropek_sin_t": np.ascontiguousarray(np.sin(ang).astype(np.float32)),
        "identm": np.eye(128, dtype=ml_dtypes.bfloat16),
        "iota_col": np.arange(128, dtype=np.float32).reshape(128, 1),
        "nibm": (np.uint32(15) << (4 * (np.arange(128, dtype=np.uint32) % 8)))
                .astype(np.uint32).view(np.int32).reshape(128, 1),
        "nibp": (2.0 ** (-4.0 * (np.arange(128) % 8))).astype(np.float32).reshape(128, 1),
    }
    k_ = np.arange(DM)
    out["fp4oh"] = _bf((k_[None, :] // 128 == np.arange(32)[:, None]))
    masks = np.zeros((128, 8 * 512), dtype=np.float32)
    ki = np.arange(128)
    absoff = 8 * (ki % 16) + ki // 16
    qi = np.arange(128)
    for d in range(8):
        valid = ((8 * qi[None, :] + c) >= (128 * d + absoff[:, None])).astype(np.float32)
        masks[:, 512 * d:512 * (d + 1)] = np.tile(valid, (1, 4))
    out["dmasks"] = _bf(masks)
    return out


def build(upto=99, nogptq=False, nocoll=False, dbg=False):
    nc = bass.Bass("TRN2", target_bir_lowering=False, debug=False)

    def din(name, shape, dt):
        return nc.dram_tensor(name, list(shape), dt, kind="ExternalInput").ap()

    X = din("x_own", (TOWN, DM), F32)
    QC = din("q_codes", (DM, DM), I32)
    KC = din("k_codes", (DM, KVH * D), I32)
    VC = din("v_codes", (DM, KVH * D), I32)
    OC = din("o_codes", (DM, DM), I32)
    QS = din("q_scales", (32, DM), F32)
    KS = din("k_scales", (32, KVH * D), F32)
    VS = din("v_scales", (32, KVH * D), F32)
    OS = din("o_scales", (32, DM), F32)
    GQW = din("g_qw", (DM // 8, ISL), I32)
    GSC = din("g_sc", (32, ISL), F32)
    GQZ = din("g_qz", (32, ISL // 8), I32)
    GGI = din("g_gi", (DM,), I32)
    UQW = din("u_qw", (DM // 8, ISL), I32)
    USC = din("u_sc", (32, ISL), F32)
    UQZ = din("u_qz", (32, ISL // 8), I32)
    UGI = din("u_gi", (DM,), I32)
    DQW = din("d_qw", (ISL // 8, DM), I32)
    DSC = din("d_sc", (86, DM), F32)
    DQZ = din("d_qz", (86, DM // 8), I32)
    DGI = din("d_gi", (ISL,), I32)
    RQC = din("ropeq_cos", (HD2, TOWN), F32)
    RQS = din("ropeq_sin", (HD2, TOWN), F32)
    RKC = din("ropek_cos_t", (TOWN, HD2), F32)
    RKS = din("ropek_sin_t", (TOWN, HD2), F32)
    IDM = din("identm", (128, 128), BF16)
    IOTA = din("iota_col", (128, 1), F32)
    NIBM = din("nibm", (128, 1), I32)
    NIBP = din("nibp", (128, 1), F32)
    FP4OH = din("fp4oh", (32, DM), BF16)
    DMASK = din("dmasks", (128, 8 * 512), BF16)

    OUT = nc.dram_tensor("out_own", [TOWN, DM], F32, kind="ExternalOutput").ap()

    KD = "ExternalOutput" if dbg else "Internal"
    q_store = nc.dram_tensor("q_store", [H * D, TOWN], BF16, kind=KD).ap()
    kin = nc.dram_tensor("kin", [TOWN, KVH * D], BF16).ap()
    vin = nc.dram_tensor("vin", [TOWN, KVH * D], BF16).ap()
    kout = [nc.dram_tensor(f"kout{r}", [NC * THALF, KVH * D], BF16).ap() for r in range(2)]
    vout = [nc.dram_tensor(f"vout{r}", [NC * THALF, KVH * D], BF16).ap() for r in range(2)]
    wg_sh = nc.dram_tensor("wg_sh", [DM, ISL], BF16).ap()
    wu_sh = nc.dram_tensor("wu_sh", [DM, ISL], BF16).ap()
    wd_sh = nc.dram_tensor("wd_sh", [ISL, DM], BF16).ap()
    wgF = nc.dram_tensor("wgF", [NC * DM, ISL], BF16).ap()
    wuF = nc.dram_tensor("wuF", [NC * DM, ISL], BF16).ap()
    wdF = nc.dram_tensor("wdF", [INTER, DM], BF16).ap()
    ao_store = nc.dram_tensor("ao_store", [TOWN, H * D], BF16, kind=KD).ap()
    s_store = nc.dram_tensor("s_store", [TOWN, DM], BF16, kind=KD).ap()
    if dbg:
        dbg_outs = []
        for nm, srcap in [("dbg_kin", kin), ("dbg_vin", vin), ("dbg_kout0", kout[0]),
                          ("dbg_kout1", kout[1]), ("dbg_vout0", vout[0]),
                          ("dbg_vout1", vout[1]),
                          ("dbg_wg_sh", wg_sh), ("dbg_wd_sh", wd_sh)]:
            o = nc.dram_tensor(nm, list(srcap.shape), BF16, kind="ExternalOutput").ap()
            dbg_outs.append((o, srcap))
        dbg_wgf = nc.dram_tensor("dbg_wgF0", [DM, ISL], BF16, kind="ExternalOutput").ap()
        dbg_xnfm = nc.dram_tensor("dbg_xnfm", [128, 32 * TOWN], BF16, kind="ExternalOutput").ap()
        dbg_xnt = nc.dram_tensor("dbg_xnt", [TOWN, DM], BF16, kind="ExternalOutput").ap()
        dbg_sts = nc.dram_tensor("dbg_sts", [128, 512], F32, kind="ExternalOutput").ap()
        dbg_eps = nc.dram_tensor("dbg_eps", [128, 16], F32, kind="ExternalOutput").ap()

    ENGS = ("sync", "gpsimd", "vector", "scalar", "tensor")
    prog = {e: [] for e in ENGS}
    ctr = {}
    sems = {}
    es = ExitStack()
    LDL = [f"LD{i}" for i in range(16)]
    STL = [f"ST{i}" for i in range(8)]
    GPL = [f"GP{i}" for i in range(4)]
    for sname in LDL + STL + GPL + ["PE", "VE", "AC", "PO"]:
        sems[sname] = es.enter_context(nc.semaphore(sname))
        ctr[sname] = 0
    lane_rr = {"LD": 0, "ST": 0, "GP": 0}
    LANES = {"LD": LDL, "ST": STL, "GP": GPL}
    CCNAMES = ["cK0", "cV0", "cK1", "cV1", "cWG", "cWU", "cWD"]
    for sname in CCNAMES:
        sems[sname] = es.enter_context(nc.semaphore(sname))
        ctr[sname] = 0

    def op(eng, fn, inc=None, waits=(), inck=1):
        tok = None
        if inc is not None:
            ctr[inc] += inck
            tok = ctr[inc]
        wl = tuple((sems[s], v) for (s, v) in [w for w in waits if w is not None]
                   if v and v > 0)

        def run(e, fn=fn, wl=wl, inc=inc, inck=inck):
            for s, v in wl:
                e.wait_ge(s, v)
            inst = fn(e)
            if inc is not None:
                inst.then_inc(sems[inc], inck)
        prog[eng].append(run)
        return tok

    def wend(eng, waits):
        wl = tuple((sems[s], v) for (s, v) in [w for w in waits if w is not None]
                   if v and v > 0)

        def run(e, wl=wl):
            for s, v in wl:
                e.wait_ge(s, v)
        prog[eng].append(run)

    def dma(eng, out_ap, in_ap, waits=(), sem=None):
        q = sem or {"scalar": "ST", "gpsimd": "GP"}.get(eng, "LD")
        lanes = LANES[q]
        lane = lanes[lane_rr[q] % len(lanes)]
        lane_rr[q] += 1
        waits = tuple(waits) + ((lane, ctr[lane]),)
        v = op(eng, lambda e, o=out_ap, i=in_ap: e.dma_start(out=o, in_=i),
               inc=lane, waits=waits, inck=16)
        return (lane, v)

    def dma_now(q):
        return [(ln, ctr[ln]) for ln in LANES[q] if ctr[ln] > 0]

    def dma_op(eng, fn, waits=(), q="LD"):
        lanes = LANES[q]
        lane = lanes[lane_rr[q] % len(lanes)]
        lane_rr[q] += 1
        waits = tuple(waits) + ((lane, ctr[lane]),)
        v = op(eng, fn, inc=lane, waits=waits, inck=16)
        return (lane, v)

    def barrier():
        now = [(s, ctr[s]) for s in sems if not s.startswith("c") and ctr.get(s, 0) > 0]
        for eng in ENGS:
            wend(eng, now)

    sb_guards = {}
    sb_seq = [0]

    def sbuf(name, shape, dt):
        g = nc.sbuf_tensor(name, list(shape), dt)
        t = g.__enter__()
        sb_seq[0] += 1
        sb_guards[name] = (t, g, sb_seq[0])
        return t

    def sfree(*names):
        for n in sorted(names, key=lambda n: -sb_guards[n][2]):
            sb_guards.pop(n)[1].__exit__(None, None, None)

    psA = es.enter_context(nc.psum_tensor("psA", [128, 1024], F32))
    psB = es.enter_context(nc.psum_tensor("psB", [128, 1024], F32))
    ps2 = es.enter_context(nc.psum_tensor("ps2", [128, 512], F32))
    ps3 = es.enter_context(nc.psum_tensor("ps3", [128, 512], F32))
    ps4 = es.enter_context(nc.psum_tensor("ps4", [128, 512], F32))
    ps5 = es.enter_context(nc.psum_tensor("ps5", [128, 512], F32))
    PSW = [ps2, ps3, ps4, ps5]

    xn_fm = sbuf("xn_fm", [128, 32 * TOWN], BF16)
    ident = sbuf("ident", [128, 128], BF16)
    iota_c = sbuf("iota_c", [128, 1], F32)
    junkc = sbuf("junkc", [128, 1], F32)
    fp4oh = sbuf("fp4oh_t", [32, DM], BF16)
    negB = sbuf("negB", [128, 1], F32)
    epsT = sbuf("epsT", [128, 1], F32)
    onesG = sbuf("onesG", [1, 128], BF16)
    nibm = sbuf("nibm_t", [128, 1], I32)
    nibp = sbuf("nibp_t", [128, 1], F32)
    rqc = sbuf("rqc", [HD2, TOWN], F32)
    rqs = sbuf("rqs", [HD2, TOWN], F32)
    rkc = sbuf("rkc", [128, 8 * HD2], F32)
    rks = sbuf("rks", [128, 8 * HD2], F32)
    s4 = {"q": sbuf("s4q", [32, DM], BF16), "k": sbuf("s4k", [32, KVH * D], BF16),
          "v": sbuf("s4v", [32, KVH * D], BF16), "o": sbuf("s4o", [32, DM], BF16)}

    t_consts = 0
    for dst, src in [(ident[:, :], IDM), (iota_c[:, :], IOTA), (fp4oh[:, :], FP4OH),
                     (rqc[:, :], RQC), (rqs[:, :], RQS),
                     (nibm[:, :], NIBM), (nibp[:, :], NIBP)]:
        t_consts = dma("sync", dst, src)
    dma("sync", rkc[:, :], RKC.rearrange("(tt p) h -> p tt h", p=128))
    dma("sync", rks[:, :], RKS.rearrange("(tt p) h -> p tt h", p=128))
    import os as _os
    _notab = bool(int(_os.environ.get("KNOTAB", "0")))
    t_s4tok = {}
    for key, src in ([] if _notab else [("q", QS), ("k", KS), ("v", VS), ("o", OS)]):
        t_s4tok[key] = dma("gpsimd", s4[key][:, :], src)
    op("vector", lambda e: e.memset(negB[:, :], -B_EXP))
    op("vector", lambda e: e.memset(epsT[:, :], 1e-5))
    t_gp_init = op("vector", lambda e: e.memset(onesG[:, :], 1.0), inc="VE")

    sgt = sbuf("sgt", [32, ISL], BF16)
    sut = sbuf("sut", [32, ISL], BF16)
    sdt = sbuf("sdt", [86, DM], BF16)
    zgt = sbuf("zgt", [32, ISL], BF16)
    zut = sbuf("zut", [32, ISL], BF16)
    zdt = sbuf("zdt", [86, DM], BF16)
    if not _notab:
        t_sg = dma("gpsimd", sgt[:, :], GSC)
        t_su = dma("gpsimd", sut[:, :], USC)
        t_sd = dma("gpsimd", sdt[:, :], DSC)
    zraw = sbuf("zraw", [86, DM // 8], I32)
    zint = sbuf("zint", [86, DM], I32)
    t_zg = dma("sync", zraw[0:32, 0:ISL // 8], GQZ) if not _notab else None

    def emit_zp(raw, G, NP, ztab, tld):
        for s8 in range(8):
            op("vector", lambda e, r=raw, s8=s8, G=G, NP=NP:
               e.tensor_scalar(out=zint[0:G, 0:8 * NP][:, s8::8],
                               in0=r[0:G, 0:NP],
                               scalar1=4 * s8, scalar2=15,
                               op0=ALU.logical_shift_right, op1=ALU.bitwise_and),
               waits=[tld] if s8 == 0 else ())
        return op("vector", lambda e, zt=ztab, G=G, NP=NP:
                  e.tensor_scalar_add(zt[0:G, 0:8 * NP], zint[0:G, 0:8 * NP], 1.0), inc="VE")

    if not _notab:
        t_zgt = emit_zp(zraw, 32, ISL // 8, zgt, t_zg)
        t_zu = dma("sync", zraw[0:32, 0:ISL // 8], UQZ, waits=[("VE", t_zgt)])
        t_zut = emit_zp(zraw, 32, ISL // 8, zut, t_zu)
        t_zd = dma("sync", zraw[:, :], DQZ, waits=[("VE", t_zut)])
        t_zdt = emit_zp(zraw, 86, DM // 8, zdt, t_zd)
    else:
        t_zdt = 0
    sfree("zint")

    xbuf = [sbuf(f"xbuf{i}", [128, DM], F32) for i in range(2)]
    xnt = [sbuf(f"xnt{i}", [128, DM], BF16) for i in range(2)]
    sqd = sbuf("sqd", [128, DM], F32)
    st_s = sbuf("st_s", [128, 64], F32)
    dbgep = sbuf("dbgep", [128, 16], F32) if dbg else None
    t_xn_cons = [0, 0]
    t_ev_last = [0]
    t_xnt_pe = [0, 0]
    t_stats_prev = 0

    XR = X.rearrange("(tt p) n -> tt p n", p=128)
    t_x = [dma("sync", xbuf[0][:, :], XR[0], waits=[("VE", t_zdt)]),
           dma("sync", xbuf[1][:, :], XR[1], waits=[("VE", t_zdt)])]
    for t8 in range(8):
        xb, xn = xbuf[t8 % 2], xnt[t8 % 2]
        tld = t_x[t8]
        op("vector", lambda e, xb=xb: e.reduce_sum(st_s[:, 0:1], xb[:, :], axis=AXX),
           waits=[tld])
        op("vector", lambda e, xb=xb: e.tensor_tensor(
            out=sqd[:, :], in0=xb[:, :], in1=xb[:, :], op=ALU.mult))
        op("vector", lambda e: e.reduce_sum(st_s[:, 8:9], sqd[:, :], axis=AXX))
        spc = lambda e: e.tensor_scalar_mul(st_s[:, 62:63], iota_c[:, :], 1.0)
        aspc = lambda e: e.activation(out=st_s[:, 61:62], in_=epsT[:, :], func=AF.Copy)
        op("vector", lambda e: e.tensor_scalar_mul(st_s[:, 16:17], st_s[:, 0:1], 1.0 / DM))
        op("vector", lambda e: e.tensor_scalar_mul(st_s[:, 24:25], st_s[:, 8:9], 1.0 / DM))
        op("vector", lambda e: e.tensor_tensor(out=st_s[:, 32:33], in0=st_s[:, 16:17],
                                               in1=st_s[:, 16:17], op=ALU.mult))
        op("vector", spc)
        op("vector", spc)
        t_var = op("vector", lambda e: e.tensor_tensor(out=st_s[:, 40:41], in0=st_s[:, 24:25],
                                                       in1=st_s[:, 32:33], op=ALU.subtract), inc="VE")
        if dbg and t8 == 0:
            op("scalar", lambda e: e.activation(out=dbgep[:, 2:3], in_=epsT[:, :],
                                                func=AF.Copy), waits=[("VE", t_var)])
            op("scalar", lambda e: e.activation(out=dbgep[:, 3:4], in_=st_s[:, 16:17],
                                                func=AF.Copy))
        op("scalar", aspc, waits=[("VE", t_var)])
        op("scalar", aspc)
        t_sd_ = op("scalar", lambda e: e.activation(out=st_s[:, 48:49], in_=st_s[:, 40:41],
                                                    func=AF.Sqrt, bias=epsT[:, :]),
                   inc="AC")
        op("vector", spc, waits=[("AC", t_sd_)])
        op("vector", spc)
        op("vector", lambda e: e.reciprocal(st_s[:, 56:57], st_s[:, 48:49]))
        op("vector", spc)
        op("vector", spc)
        op("vector", lambda e: e.scalar_tensor_tensor(out=st_s[:, 32:33], in0=st_s[:, 16:17],
                                                      scalar=-1.0, in1=st_s[:, 56:57],
                                                      op0=ALU.mult, op1=ALU.mult))
        op("vector", spc)
        op("vector", spc)
        t_xnv = op("vector", lambda e, xb=xb, xn=xn: e.tensor_scalar(
            out=xn[:, :], in0=xb[:, :], scalar1=st_s[:, 56:57], scalar2=st_s[:, 32:33],
            op0=ALU.mult, op1=ALU.add), inc="VE",
            waits=[("PE", t_xnt_pe[t8 % 2])])
        t_stats_prev = t_xnv
        t_xn_cons[t8 % 2] = t_xnv
        if dbg:
            dma("scalar", dbg_xnt[128 * t8:128 * (t8 + 1), :], xn[:, :],
                waits=[("VE", t_xnv)])
            dma("scalar", dbg_sts[:, 64 * t8:64 * (t8 + 1)], st_s[:, :],
                waits=[("VE", t_xnv)])

        if t8 + 2 < 8:
            t_x.append(dma("sync", xbuf[t8 % 2][:, :], XR[t8 + 2], waits=[("VE", t_xnv)]))
        TRS = [(psA, 0), (psA, 512), (psB, 0), (psB, 512)]
        for g4 in range(8):
            for s2 in range(4):
                kc = 4 * g4 + s2
                bank, off = TRS[s2]
                w0 = []
                if s2 == 0:
                    w0 = [("VE", t_ev_last[0]), ("VE", t_xnv)]
                op("tensor", lambda e, xn=xn, kc=kc, bank=bank, off=off:
                   e.transpose(bank[:, off:off + 64].bitcast(BF16),
                               xn[:, 128 * kc:128 * (kc + 1)], ident[:, :]),
                   inc="PE" if s2 == 3 else None, waits=w0)
            t_tr = ctr["PE"]
            for s2 in range(4):
                kc = 4 * g4 + s2
                bank, off = TRS[s2]
                t_ev = op("vector", lambda e, bank=bank, off=off, kc=kc, t8=t8:
                          e.tensor_copy(
                              out=xn_fm[:, kc * TOWN + 128 * t8: kc * TOWN + 128 * (t8 + 1)],
                              in_=bank[:, off:off + 64].bitcast(BF16)),
                          inc="VE", waits=[("PE", t_tr)] if s2 == 0 else ())
            t_ev_last[0] = t_ev
        t_xnt_pe[t8 % 2] = ctr["PE"]
    if dbg:
        op("vector", lambda e: e.tensor_copy(out=dbgep[:, 0:1], in_=epsT[:, :]))
        op("vector", lambda e: e.tensor_copy(out=dbgep[:, 1:2], in_=negB[:, :]))
        t_dbgep = op("vector", lambda e: e.engine_nop(), inc="VE")
        dma("scalar", dbg_eps, dbgep[:, :], waits=[("VE", t_dbgep)])
    t_xn_done = op("vector", lambda e: e.engine_nop(), inc="VE")
    sfree("xbuf0", "xbuf1", "xnt0", "xnt1", "sqd", "st_s",
          *(["dbgep"] if dbg else []))

    if upto < 1:
        if dbg:
            wend("sync", dma_now("ST") + dma_now("LD") + [("VE", ctr["VE"])])
            dma("sync", dbg_xnfm, xn_fm[:, :])
            wend("sync", dma_now("LD"))
        return _finish(nc, prog, ctr, sems, wend, es, CCNAMES[:0])
    cdb = [sbuf(f"cdb{i}", [128, 32 * 128], I32) for i in range(2)]
    wst = [sbuf(f"wst{i}", [128, 32 * 128], BF16) for i in range(2)]
    qev = sbuf("qev", [128, 1024], BF16)
    rtmp1 = sbuf("rtmp1", [64, 512], F32)
    rtmp2 = sbuf("rtmp2", [64, 512], F32)
    kr1 = sbuf("kr1", [128, 64], F32)
    kr2 = sbuf("kr2", [128, 64], F32)
    kvst = sbuf("kvst", [128, 1024], BF16)
    pkb = [sbuf(f"pkb{i}", [128, 512], I32) for i in range(2)]
    ut1 = sbuf("ut1", [128, 512], I32)
    ut2f = [sbuf(f"ut2{i}", [128, 512], F32) for i in range(2)]
    ut3 = sbuf("ut3", [128, 512], F32)
    wdq = [sbuf(f"wdq{i}", [128, 512], BF16) for i in range(2)]
    gixb = sbuf("gixb", [1, 128], BF16)
    gixi = sbuf("gixi", [1, 128], I32)
    ohb = sbuf("ohb", [86, 128], BF16)

    t_cdb = [t_xn_done, t_xn_done]
    t_wst_cons = [0, 0]
    t_qev_cons = None
    t_kvst_cons = None
    t_ps23_cons = [t_ev_last[0], t_ev_last[0]]



    gptq_units = []

    def plan_gptq_shard(qw_dram, gi_dram, stab, ztab, G, rows, cols, out_dram, tsld, tzld):
        nrc = (rows + 127) // 128
        csl = []
        c0 = 0
        while c0 < cols:
            cw = min(512, cols - c0)
            csl.append((c0, cw))
            c0 += cw
        for rc in range(nrc):
            w = min(128, rows - 128 * rc)
            gptq_units.append(("oh", qw_dram, gi_dram, stab, ztab, G, rc, w, None,
                               out_dram, tsld, tzld))
            for cs in csl:
                gptq_units.append(("tile", qw_dram, gi_dram, stab, ztab, G, rc, w, cs,
                                   out_dram, tsld, tzld))

    plan_gptq_shard(GQW, GGI, sgt, zgt, 32, DM, ISL, wg_sh, t_sg, t_zgt)
    plan_gptq_shard(UQW, UGI, sut, zut, 32, DM, ISL, wu_sh, t_su, t_zut)
    plan_gptq_shard(DQW, DGI, sdt, zdt, 86, ISL, DM, wd_sh, t_sd, t_zdt)

    gp = {"i": 0, "par": 0, "t_oh": 0, "t_gix_cons": 0, "t_pk": [0, 0],
          "t_ut2_cons": [0, 0], "t_w_cons": [None, None], "t_chain": t_xn_done, "t_store": None,
          "t_ut1_cons": 0}

    def emit_gptq_some(n):
        for _ in range(n):
            if gp["i"] >= len(gptq_units):
                return
            kind, qw, gi, stab, ztab, G, rc, w, cs, odram, tsld, tzld = gptq_units[gp["i"]]
            gp["i"] += 1
            if kind == "oh":
                tld = dma("sync", gixi[0:1, 0:w],
                          gi.rearrange("(o n) -> o n", o=1)[:, 128 * rc:128 * rc + w],
                          waits=[("PE", gp["t_gix_cons"]), ("VE", t_xn_done)])
                t_cv = op("vector", lambda e, w=w: e.tensor_copy(
                    out=gixb[0:1, 0:w], in_=gixi[0:1, 0:w]),
                    inc="VE", waits=[tld, ("VE", gp["t_oh"])])
                t_bc = op("tensor", lambda e, w=w, G=G: e.matmul(
                    psA[0:G, 0:w], onesG[:, 0:G], gixb[0:1, 0:w], start=True, stop=True),
                    inc="PE", waits=[("VE", t_cv), ("VE", t_gp_init), ("VE", gp["t_chain"])])
                gp["t_gix_cons"] = t_bc
                gp["t_oh"] = op("vector", lambda e, G=G, w=w: e.tensor_scalar(
                    out=ohb[0:G, 0:w], in0=psA[0:G, 0:w],
                    scalar1=iota_c[0:G, :], scalar2=None, op0=ALU.is_equal),
                    inc="VE", waits=[("PE", t_bc)])
            else:
                c0, cw = cs
                par = gp["par"]
                gp["par"] ^= 1
                rpn = w // 8
                tpk = dma("sync", pkb[par][0:w, 0:cw],
                          qw[16 * rc:16 * rc + rpn, c0:c0 + cw]
                          .rearrange("r (o n) -> r o n", o=1).broadcast_to([rpn, 8, cw]),
                          waits=[("VE", gp["t_pk"][par]), ("VE", t_xn_done)])
                op("tensor", lambda e, G=G, w=w, stab=stab, c0=c0, cw=cw:
                   e.matmul(psA[0:w, 0:cw], ohb[0:G, 0:w], stab[0:G, c0:c0 + cw],
                            start=True, stop=True),
                   waits=[("VE", gp["t_oh"]), ("VE", gp["t_chain"]), tsld])
                t_g = op("tensor", lambda e, G=G, w=w, ztab=ztab, c0=c0, cw=cw:
                         e.matmul(psA[0:w, 512:512 + cw], ohb[0:G, 0:w], ztab[0:G, c0:c0 + cw],
                                  start=True, stop=True),
                         inc="PE", waits=[("VE", tzld)])
                t_u1 = op("vector", lambda e, par=par, w=w, cw=cw: e.tensor_scalar(
                    out=ut1[0:w, 0:cw].bitcast(U32INT), in0=pkb[par][0:w, 0:cw].bitcast(U32INT),
                    scalar1=nibm[0:w, :].bitcast(U32INT),
                    scalar2=None, op0=ALU.bitwise_and),
                    inc="VE",
                    waits=[tpk, t_consts])
                t_u2 = op("vector", lambda e, par=par, w=w, cw=cw: e.tensor_scalar(
                    out=ut2f[par][0:w, 0:cw], in0=ut1[0:w, 0:cw].bitcast(U32INT),
                    scalar1=nibp[0:w, :],
                    scalar2=None, op0=ALU.mult), inc="VE",
                    waits=[("VE", gp["t_ut2_cons"][par])])
                gp["t_ut1_cons"] = t_u2
                gp["t_pk"][par] = t_u1
                t_u3 = op("vector", lambda e, par=par, w=w, cw=cw: e.tensor_tensor(
                    out=ut3[0:w, 0:cw], in0=ut2f[par][0:w, 0:cw], in1=psA[0:w, 512:512 + cw],
                    op=ALU.subtract),
                    waits=[("PE", t_g), gp["t_w_cons"][par]])
                gp["t_ut2_cons"][par] = ctr["VE"] + 1
                gp["t_chain"] = op("vector", lambda e, par=par, w=w, cw=cw: e.tensor_tensor(
                    out=wdq[par][0:w, 0:cw], in0=ut3[0:w, 0:cw], in1=psA[0:w, 0:cw],
                    op=ALU.mult), inc="VE")
                gp["t_ut2_cons"][par] = gp["t_chain"]
                tst = dma("scalar", odram[128 * rc:128 * rc + w, c0:c0 + cw],
                          wdq[par][0:w, 0:cw], waits=[("VE", gp["t_chain"])])
                gp["t_w_cons"][par] = tst
                gp["t_store"] = tst

    fp4_prev_toks = [None]

    def emit_fp4_head(codes_dram, stab, col0, par, t_stab=0):
        tld = dma("sync", cdb[par][:, :],
                  codes_dram[:, col0:col0 + 128].rearrange("(kc p) n -> p kc n", p=128),
                  waits=[("VE", t_cdb[par])])
        toks = []
        prev = fp4_prev_toks[0]
        for kc in range(32):
            slot = 512 * (kc % 2)
            if kc >= 2:
                wg_ = [("VE", toks[kc - 2])]
            elif prev is not None:
                wg_ = [("VE", prev[kc + 30])]
            else:
                wg_ = [("VE", t_xn_done)]
            if kc == 0:
                wg_ = list(wg_) + [t_stab]
            t_g = op("tensor", lambda e, kc=kc, slot=slot, stab=stab, col0=col0:
                     e.matmul(psB[:, slot:slot + 128],
                              fp4oh[:, 128 * kc:128 * (kc + 1)], stab[:, col0:col0 + 128],
                              start=True, stop=True),
                     inc="PE", waits=wg_)
            wd_ = [("PE", t_g), tld]
            if kc == 0:
                wd_.append(("PE", t_wst_cons[par]))
                wd_.append(("VE", t_xn_done))
            t_dq = op("vector", lambda e, kc=kc, slot=slot, par=par:
                      e.scalar_tensor_tensor(
                          out=wst[par][:, 128 * kc:128 * (kc + 1)],
                          in0=cdb[par][:, 128 * kc:128 * (kc + 1)], scalar=8.0,
                          in1=psB[:, slot:slot + 128],
                          op0=ALU.subtract, op1=ALU.mult),
                      inc="VE", waits=wd_)
            toks.append(t_dq)
        t_cdb[par] = toks[-1]
        fp4_prev_toks[0] = toks
        return toks

    import os as _os
    _hsel = _os.environ.get("KHEADS", "all")
    nheads = [("q", h) for h in range(H)] + [("k", j) for j in range(KVH)] + \
             [("v", j) for j in range(KVH)]
    if _hsel == "q":
        nheads = nheads[:H]
    elif _hsel == "kv":
        nheads = nheads[H:]
    elif _hsel == "q1":
        nheads = nheads[:1]
    elif _hsel == "kv1":
        nheads = nheads[H:H + 1]
    t_kin_st = 0
    t_vin_st = 0
    t_q_st = []
    par = 0
    for (kind, h) in nheads:
        codes = {"q": QC, "k": KC, "v": VC}[kind]
        dq_toks = emit_fp4_head(codes, s4[kind], 128 * h, par, t_s4tok[kind])
        if kind == "q":
            t_mm = {}
            for half in range(2):
                bank = [ps2, ps3][half]
                for kc in range(32):
                    wts = [("VE", dq_toks[min(kc + 2, 31)])]
                    if kc == 0:
                        wts += [("VE", t_ps23_cons[half]), ("VE", t_xn_done)]
                    t_mm[half] = op("tensor", lambda e, kc=kc, bank=bank, half=half, par=par:
                                    e.matmul(bank[:, :],
                                             wst[par][:, 128 * kc:128 * (kc + 1)],
                                             xn_fm[:, kc * TOWN + 512 * half:kc * TOWN + 512 * (half + 1)],
                                             start=(kc == 0), stop=(kc == 31)),
                                    inc="PE" if kc == 31 else None, waits=wts)
            t_wst_cons[par] = t_mm[1]
            t_r = 0
            for half in range(2):
                bank = [ps2, ps3][half]
                cqs = rqc[:, 512 * half:512 * (half + 1)]
                sns = rqs[:, 512 * half:512 * (half + 1)]
                op("vector", lambda e, bank=bank, cqs=cqs: e.tensor_tensor(
                    out=rtmp1[:, :], in0=bank[0:64, :], in1=cqs, op=ALU.mult),
                    waits=[("PE", t_mm[half]), t_qev_cons])
                op("vector", lambda e, bank=bank, sns=sns: e.tensor_tensor(
                    out=rtmp2[:, :], in0=bank[64:128, :], in1=sns, op=ALU.mult))
                op("vector", lambda e, half=half: e.tensor_tensor(
                    out=qev[0:64, 512 * half:512 * (half + 1)], in0=rtmp1[:, :],
                    in1=rtmp2[:, :], op=ALU.subtract))
                op("vector", lambda e, bank=bank, cqs=cqs: e.tensor_tensor(
                    out=rtmp1[:, :], in0=bank[64:128, :], in1=cqs, op=ALU.mult))
                op("vector", lambda e, bank=bank, sns=sns: e.tensor_tensor(
                    out=rtmp2[:, :], in0=bank[0:64, :], in1=sns, op=ALU.mult))
                t_r = op("vector", lambda e, half=half: e.tensor_tensor(
                    out=qev[64:128, 512 * half:512 * (half + 1)], in0=rtmp1[:, :],
                    in1=rtmp2[:, :], op=ALU.add), inc="VE")
                t_ps23_cons[half] = t_r
            t_qev_cons = dma("scalar", q_store[128 * h:128 * (h + 1), :], qev[:, :],
                             waits=[("VE", t_r)])
            t_q_st.append(t_qev_cons)
        else:
            for tt in range(8):
                bank = [ps2, ps3][tt % 2]
                for kc in range(32):
                    wts = [("VE", dq_toks[min(kc + 2, 31)])]
                    if kc == 0:
                        wts += [("VE", t_ps23_cons[tt % 2]), ("VE", t_xn_done)]
                    tk = op("tensor", lambda e, kc=kc, tt=tt, bank=bank, par=par:
                            e.matmul(bank[:, 0:128],
                                     xn_fm[:, kc * TOWN + 128 * tt:kc * TOWN + 128 * (tt + 1)],
                                     wst[par][:, 128 * kc:128 * (kc + 1)],
                                     start=(kc == 0), stop=(kc == 31)),
                            inc="PE" if kc == 31 else None, waits=wts)
                cqs = rkc[:, 64 * tt:64 * (tt + 1)]
                sns = rks[:, 64 * tt:64 * (tt + 1)]
                if kind == "k":
                    op("vector", lambda e, bank=bank, cqs=cqs: e.tensor_tensor(
                        out=kr1[:, :], in0=bank[:, 0:64], in1=cqs, op=ALU.mult),
                        waits=[("PE", tk), t_kvst_cons])
                    op("vector", lambda e, bank=bank, sns=sns: e.tensor_tensor(
                        out=kr2[:, :], in0=bank[:, 64:128], in1=sns, op=ALU.mult))
                    op("vector", lambda e: e.tensor_scalar_mul(junkc[:, :], iota_c[:, :], 1.0))
                    op("vector", lambda e, tt=tt: e.tensor_tensor(
                        out=kvst[:, 128 * tt:128 * tt + 64], in0=kr1[:, :],
                        in1=kr2[:, :], op=ALU.subtract))
                    op("vector", lambda e, bank=bank, cqs=cqs: e.tensor_tensor(
                        out=kr1[:, :], in0=bank[:, 64:128], in1=cqs, op=ALU.mult))
                    op("vector", lambda e, bank=bank, sns=sns: e.tensor_tensor(
                        out=kr2[:, :], in0=bank[:, 0:64], in1=sns, op=ALU.mult))
                    op("vector", lambda e: e.tensor_scalar_mul(junkc[:, :], iota_c[:, :], 1.0))
                    t_e = op("vector", lambda e, tt=tt: e.tensor_tensor(
                        out=kvst[:, 128 * tt + 64:128 * (tt + 1)], in0=kr1[:, :],
                        in1=kr2[:, :], op=ALU.add), inc="VE")
                else:
                    t_e = op("vector", lambda e, bank=bank, tt=tt: e.tensor_copy(
                        out=kvst[:, 128 * tt:128 * (tt + 1)], in_=bank[:, 0:128]),
                        inc="VE", waits=[("PE", tk), t_kvst_cons])
                t_ps23_cons[tt % 2] = t_e
            t_wst_cons[par] = tk
            op("vector", lambda e: e.tensor_scalar_mul(junkc[:, :], iota_c[:, :], 1.0))
            op("vector", lambda e: e.tensor_scalar_mul(junkc[:, :], iota_c[:, :], 1.0))
            t_kv_settle = op("vector", lambda e: e.engine_nop(), inc="VE")
            dest = kin if kind == "k" else vin
            for tt in range(8):
                t_kvst_cons = dma("scalar",
                                  dest[128 * tt:128 * (tt + 1), 128 * h:128 * (h + 1)],
                                  kvst[:, 128 * tt:128 * (tt + 1)],
                                  waits=([("VE", t_kv_settle)] if tt == 0 else ()))
            if kind == "k":
                t_kin_st = t_kvst_cons
            else:
                t_vin_st = t_kvst_cons
        par ^= 1
        if not nogptq:
            emit_gptq_some(8)
    if not nogptq:
        emit_gptq_some(len(gptq_units))

    RG = [list(range(NC))]

    def coll(sem, in_ap, out_ap, waits):
        op("gpsimd", lambda e, i=in_ap, o=out_ap: e.collective_compute(
            "AllGather", ALU.bypass, replica_groups=RG, ins=[i.opt()], outs=[o.opt()]),
            inc=sem, waits=waits)

    if not nocoll:
        coll("cK0", kin[0:THALF, :], kout[0], dma_now("ST"))
        coll("cV0", vin[0:THALF, :], vout[0], [])
        coll("cK1", kin[THALF:TOWN, :], kout[1], [])
        coll("cV1", vin[THALF:TOWN, :], vout[1], [])
        if not nogptq:
            coll("cWG", wg_sh[:, :], wgF, dma_now("ST"))
            coll("cWU", wu_sh[:, :], wuF, [])
            coll("cWD", wd_sh[:, :], wdF, [])

    sfree("sgt", "sut", "sdt", "zgt", "zut", "zdt", "zraw",
          "pkb0", "pkb1", "ut1", "ut20", "ut21", "ut3", "wdq0", "wdq1",
          "cdb0", "cdb1", "qev", "kvst", "gixb", "gixi", "ohb",
          "rtmp1", "rtmp2", "kr1", "kr2", "wst0", "wst1")

    if upto < 2:
        if dbg:
            wts = dma_now("ST") + dma_now("LD") + [("VE", ctr["VE"])] +                   ([] if nocoll else [(s, 1) for s in
                                      (CCNAMES[:4] if nogptq else CCNAMES)])
            wend("sync", wts)
            for o, srcap in dbg_outs:
                dma("sync", o, srcap)
            if not nogptq:
                dma("sync", dbg_wgf, wgF[0:DM, :])
            dma("sync", dbg_xnfm, xn_fm[:, :])
            wend("sync", dma_now("LD"))
        return _finish(nc, prog, ctr, sems, wend, es,
                       [] if nocoll else (CCNAMES[:4] if nogptq else CCNAMES))
    barrier()
    dmask = sbuf("dmask", [128, 8 * 512], BF16)
    t_dmask = dma("sync", dmask[:, :], DMASK)
    kfm = sbuf("kfm", [128, 64 * 128], BF16)
    vaug = sbuf("vaug", [128, 64 * 132], BF16)
    qbuf = [sbuf(f"qbuf{i}", [128, 8 * 512], BF16) for i in range(2)]
    pbuf = [sbuf(f"pbuf{i}", [128, 1024], BF16) for i in range(4)]
    ktmp = [sbuf(f"ktmp{i}", [128, 128], BF16) for i in range(2)]
    aofm4 = [sbuf(f"aofm4{i}", [128, 512], BF16) for i in range(2)]
    nrec4 = [sbuf(f"nrec4{i}", [128, 4], F32) for i in range(2)]
    ACC = [ps2, ps3, ps4, ps5]

    op("vector", lambda e: e.memset(vaug[:, :], 0.0))
    for kt in range(64):
        op("vector", lambda e, kt=kt: e.memset(vaug[:, 132 * kt + 128:132 * kt + 129], 4.0))
    t_vones = op("vector", lambda e: e.engine_nop(), inc="VE")

    t_kfm_cons = 0
    t_vaug_cons = 0
    t_q_cons = [0, 0]
    t_pb_cons = [0, 0, 0, 0]
    t_sc_cons = [0, 0]
    t_acc_ev4 = [0, 0, 0, 0]
    t_tr_ev = [0, 0]
    t_ktmp_cons = [0, 0]
    t_ao_dma = [None, None]

    for j in range(KVH):
        t_kfm_prev = t_kfm_cons
        t_vaug_prev = t_vaug_cons
        qp = j % 2
        for kt in range(64):
            r = kt // 32
            ktl = kt - 32 * r
            src_ = kout[r][:, 128 * j:128 * (j + 1)] \
                .rearrange("(s i) d -> s i d", s=NC)[:, 16 * ktl:16 * ktl + 16, :]
            tld = dma("sync", ktmp[kt % 2][:, :], src_,
                      waits=[("PE", t_ktmp_cons[kt % 2]), ("cK%d" % r, 1)])
            bank = [psA, psB][kt % 2]
            t_tr = op("tensor", lambda e, kt=kt, bank=bank: e.transpose(
                bank[:, 0:64].bitcast(BF16), ktmp[kt % 2][:, :], ident[:, :]),
                inc="PE",
                waits=[tld, ("VE", t_tr_ev[kt % 2])] +
                      ([("AC", t_sc_cons[kt % 2])] if kt < 2 else []))
            t_ktmp_cons[kt % 2] = t_tr
            t_ev = op("vector", lambda e, kt=kt, bank=bank: e.tensor_copy(
                out=kfm[:, 128 * kt:128 * (kt + 1)], in_=bank[:, 0:64].bitcast(BF16)),
                inc="VE", waits=[("PE", t_tr), ("PE", t_kfm_prev)])
            t_tr_ev[kt % 2] = t_ev
            vsrc = vout[r][:, 128 * j:128 * (j + 1)] \
                .rearrange("(s i) d -> s i d", s=NC)[:, 16 * ktl:16 * ktl + 16, :]
            dma("sync", vaug[:, 132 * kt:132 * kt + 128], vsrc,
                waits=[("PE", t_vaug_prev), ("VE", t_vones), ("cV%d" % r, 1)])
        t_vld = dma_now("LD")
        t_kv_ready = op("vector", lambda e: e.engine_nop(), inc="VE", waits=t_vld)
        t_ql = None
        for h4 in range(4):
            hq = 4 * j + h4
            t_ql = dma("sync",
                       qbuf[qp][:, :].rearrange("p (t h q) -> p t h q", h=4, q=128)[:, :, h4, :],
                       q_store[128 * hq:128 * (hq + 1), :]
                       .rearrange("p (t q) -> p t q", q=128),
                       waits=[("PE", t_q_cons[qp]), t_q_st[hq]] if h4 == 0 else
                             [t_q_st[hq]])

        pend = [None]

        def flush_pend():
            if pend[0] is None:
                return None
            pp, p8, t, wtok, is_ac = pend[0]
            pend[0] = None
            t_pv = None
            npair = 4 * (t + 1)
            for ktp in range(2):
                kt = 2 * p8 + ktp
                for h4 in range(4):
                    wpv = []
                    if ktp == 0 and h4 == 0:
                        wpv = [(("AC" if is_ac else "VE"), wtok)]
                    if p8 == 0 and ktp == 0:
                        wpv.append(("AC", t_acc_ev4[h4]))
                    t_pv = op("tensor", lambda e, kt=kt, h4=h4, pp=pp, p8=p8, t=t, ktp=ktp, npair=npair:
                              e.matmul(ACC[h4][:, 0:132],
                                       pbuf[pp][:, 512 * ktp + 128 * h4:512 * ktp + 128 * (h4 + 1)],
                                       vaug[:, 132 * kt:132 * (kt + 1)],
                                       start=(p8 == 0 and ktp == 0),
                                       stop=(p8 == npair - 1 and ktp == 1),
                                       skip_group_check=True),
                              inc="PE" if (ktp == 1 and h4 == 3) else None, waits=wpv)
            t_pb_cons[pp] = t_pv
            return t_pv

        for t in range(8):
            npair = 4 * (t + 1)
            for p8 in range(npair):
                pp = p8 % 4
                pq = p8 % 2
                scb = [psA, psB][pq]
                t_sc = None
                for ktp in range(2):
                    kt = 2 * p8 + ktp
                    wts = []
                    if ktp == 0:
                        wts = [("AC", t_sc_cons[pq]), t_ql,
                               ("VE", t_kv_ready)]
                    t_sc = op("tensor", lambda e, kt=kt, ktp=ktp, scb=scb, qp=qp, t=t:
                              e.matmul(scb[:, 512 * ktp:512 * (ktp + 1)],
                                       kfm[:, 128 * kt:128 * (kt + 1)],
                                       qbuf[qp][:, 512 * t:512 * (t + 1)],
                                       start=True, stop=True),
                              inc="PE" if ktp == 1 else None, waits=wts)
                t_kfm_cons = t_sc
                t_vaug_cons_prev = flush_pend()
                if t_vaug_cons_prev:
                    t_vaug_cons = t_vaug_cons_prev
                t_exp = op("scalar", lambda e, scb=scb, pp=pp: e.activation(
                    out=pbuf[pp][:, :], in_=scb[:, :], func=AF.Exp, bias=negB[:, :]),
                    inc="AC", waits=[("PE", t_sc), ("PE", t_pb_cons[pp]),
                                     ("VE", t_gp_init)])
                t_sc_cons[pq] = t_exp
                if p8 >= 4 * t:
                    d0 = 2 * (p8 - 4 * t)
                    op("vector", lambda e, pp=pp, d0=d0: e.tensor_tensor(
                        out=pbuf[pp][:, 0:512], in0=pbuf[pp][:, 0:512],
                        in1=dmask[:, 512 * d0:512 * (d0 + 1)], op=ALU.mult),
                        waits=[("AC", t_exp), t_dmask])
                    t_mask = op("vector", lambda e, pp=pp, d0=d0: e.tensor_tensor(
                        out=pbuf[pp][:, 512:1024], in0=pbuf[pp][:, 512:1024],
                        in1=dmask[:, 512 * (d0 + 1):512 * (d0 + 2)], op=ALU.mult),
                        inc="VE")
                    pend[0] = (pp, p8, t, t_mask, False)
                else:
                    pend[0] = (pp, p8, t, t_exp, True)
            t_pv = flush_pend()
            t_vaug_cons = t_pv
            ppar = t % 2
            op("vector", lambda e: e.tensor_scalar_mul(junkc[:, :], iota_c[:, :], 1.0),
               waits=[("PE", t_pv), ("AC", t_acc_ev4[0]), ("AC", t_acc_ev4[1]),
                      ("AC", t_acc_ev4[2]), ("AC", t_acc_ev4[3])])
            op("vector", lambda e: e.tensor_scalar_mul(junkc[:, :], iota_c[:, :], 1.0))
            t_rc = None
            for h4 in range(4):
                t_rc = op("vector", lambda e, h4=h4, ppar=ppar: e.reciprocal(
                    nrec4[ppar][:, h4:h4 + 1], ACC[h4][:, 128:129]),
                    inc="VE" if h4 == 3 else None)
            for h4 in range(4):
                t_ae = op("scalar", lambda e, h4=h4, ppar=ppar: e.activation(
                    out=aofm4[ppar][:, 128 * h4:128 * (h4 + 1)], in_=ACC[h4][:, 0:128],
                    func=AF.Copy, scale=nrec4[ppar][:, h4:h4 + 1]),
                    inc="AC",
                    waits=[("VE", t_rc)] + ([t_ao_dma[ppar]] if h4 == 0 else []))
                t_acc_ev4[h4] = t_ae
            t_ao_dma[ppar] = dma("scalar",
                                 ao_store[128 * t:128 * (t + 1), 512 * j:512 * (j + 1)],
                                 aofm4[ppar][:, :], waits=[("AC", t_ae)])
        t_q_cons[qp] = t_pv
    t_ao_done = [t for t in t_ao_dma if t]

    sfree("kfm", "vaug", "qbuf0", "qbuf1", "ktmp0", "ktmp1",
          "aofm40", "aofm41", "nrec40", "nrec41", "pbuf0", "pbuf1",
          "pbuf2", "pbuf3")

    if upto < 3:
        return _finish(nc, prog, ctr, sems, wend, es, CCNAMES)
    barrier()
    wosl = sbuf("wosl", [128, 32 * 512], BF16)
    ocb = [sbuf(f"ocb{i}", [128, 512], I32) for i in range(2)]
    aob = [sbuf(f"aob{i}", [128, 512], BF16) for i in range(4)]
    sev = [sbuf(f"sev{i}", [128, 512], BF16) for i in range(2)]
    t_oc_cons = [0, 0]
    t_aob_cons = [0, 0, 0, 0]
    t_wo_cons = 0
    t_sev_dma = [None, None]
    t_wv_ev = [0, 0, 0, 0]

    dq3_prev = None
    for g8 in range(8):
        dq3 = []
        for kc in range(32):
            pr = kc % 2
            tld = dma("sync", ocb[pr][:, :], OC[128 * kc:128 * (kc + 1), 512 * g8:512 * (g8 + 1)],
                      waits=[("VE", t_oc_cons[pr])])
            if kc >= 2:
                wg_ = [("VE", dq3[kc - 2])]
            elif dq3_prev is not None:
                wg_ = [("VE", dq3_prev[kc + 30])]
            else:
                wg_ = [t_s4tok["o"]]
            t_g = op("tensor", lambda e, kc=kc, pr=pr, g8=g8: e.matmul(
                psA[:, 512 * pr:512 * (pr + 1)], fp4oh[:, 128 * kc:128 * (kc + 1)],
                s4["o"][:, 512 * g8:512 * (g8 + 1)], start=True, stop=True),
                inc="PE", waits=wg_)
            t_dq = op("vector", lambda e, kc=kc, pr=pr:
                      e.scalar_tensor_tensor(out=wosl[:, 512 * kc:512 * (kc + 1)],
                                             in0=ocb[pr][:, :], scalar=8.0,
                                             in1=psA[:, 512 * pr:512 * (pr + 1)],
                                             op0=ALU.subtract, op1=ALU.mult),
                      inc="VE", waits=[("PE", t_g), tld])
            t_oc_cons[pr] = t_dq
            dq3.append(t_dq)
        dq3_prev = dq3
        for wave in range(2):
            t_mm = None
            for kc in range(32):
                pr = kc % 4
                tld = dma_op("sync", lambda e, pr=pr, kc=kc, wave=wave: e.dma_start_transpose(
                    aob[pr][:, :],
                    ao_store[512 * wave:512 * (wave + 1), 128 * kc:128 * (kc + 1)]),
                    waits=[("PE", t_aob_cons[pr])] + t_ao_done)
                for tt in range(4):
                    wts = []
                    if tt == 0:
                        wts = [tld, ("VE", dq3[kc])]
                    if kc == 0:
                        wts.append(("AC", t_wv_ev[tt]))
                    t_mm = op("tensor", lambda e, kc=kc, tt=tt, pr=pr:
                              e.matmul(PSW[tt][:, :],
                                       aob[pr][:, 128 * tt:128 * (tt + 1)],
                                       wosl[:, 512 * kc:512 * (kc + 1)],
                                       start=(kc == 0), stop=(kc == 31)),
                              inc="PE" if tt == 3 else None, waits=wts)
                t_aob_cons[pr] = t_mm
            t_wo_cons = t_mm
            for tt in range(4):
                pr2 = tt % 2
                t_e = op("scalar", lambda e, tt=tt, pr2=pr2: e.activation(
                    out=sev[pr2][:, :], in_=PSW[tt][:, :], func=AF.Copy),
                    inc="AC", waits=[("PE", t_mm), t_sev_dma[pr2]])
                t_wv_ev[tt] = t_e
                t_sev_dma[pr2] = dma(
                    "scalar",
                    s_store[:, 512 * g8:512 * (g8 + 1)]
                    .rearrange("(w tt p) n -> w tt p n", w=2, p=128)[wave, tt],
                    sev[pr2][:, :], waits=[("AC", t_e)])
    t_s_done = [t for t in t_sev_dma if t]

    sfree("wosl", "ocb0", "ocb1", "aob0", "aob1", "aob2", "aob3", "sev0", "sev1")

    if upto < 4:
        return _finish(nc, prog, ctr, sems, wend, es, CCNAMES)
    barrier()
    sfree("s4q", "s4k", "s4v", "s4o", "dmask", "rqc", "rqs", "rkc", "rks",
          "ident", "iota_c", "junkc", "fp4oh_t", "negB", "onesG", "epsT",
          "nibm_t", "nibp_t")
    hhalf = sbuf("hhalf", [128, 88 * 512], BF16)
    wgb = [sbuf(f"wgb{i}", [128, 32 * 128], BF16) for i in range(2)]
    wub = [sbuf(f"wub{i}", [128, 32 * 128], BF16) for i in range(2)]
    hg = sbuf("hg", [128, 512], BF16)
    wdb = [sbuf(f"wdb{i}", [128, 512], BF16) for i in range(4)]
    gts = sbuf("gts", [128, 512], F32)
    xpb = [sbuf(f"xpb{i}", [128, 512], F32) for i in range(2)]
    spb = [sbuf(f"spb{i}", [128, 512], BF16) for i in range(2)]
    oub = [sbuf(f"oub{i}", [128, 512], F32) for i in range(2)]
    t_wg_cons = [0, 0]
    t_wu_cons = [0, 0]
    t_h_cons = 0
    t_wd_cons = [0, 0, 0, 0]
    t_xp_cons = [0, 0]
    t_sp_cons = [0, 0]
    t_ou_dma = [None, None]
    t_gu_ev = [(0, 0), (0, 0)]
    t_gts_cons = 0
    t_down_ev = [0, 0, 0, 0]

    for hh in range(2):
        t_hm = 0
        for s8 in range(8):
            for ic in range(NIC):
                w = IW[ic]
                u = s8 * NIC + ic
                pr = u % 2
                gb = [psA, psB][pr]
                tldg = dma("sync", wgb[pr][:, 0:32 * w],
                           wgF[DM * s8:DM * (s8 + 1), IOFF[ic]:IOFF[ic] + w]
                           .rearrange("(kc p) i -> p kc i", p=128),
                           waits=[("PE", t_wg_cons[pr]), ("cWG", 1)])
                tldu = dma("sync", wub[pr][:, 0:32 * w],
                           wuF[DM * s8:DM * (s8 + 1), IOFF[ic]:IOFF[ic] + w]
                           .rearrange("(kc p) i -> p kc i", p=128),
                           waits=[("PE", t_wu_cons[pr]), ("cWU", 1)])
                t_mg = t_mu = None
                for kc in range(32):
                    wts = [tldg, ("AC", t_gu_ev[pr][0]),
                           ("VE", t_gu_ev[pr][1])] if kc == 0 else []
                    t_mg = op("tensor", lambda e, kc=kc, w=w, pr=pr, gb=gb, hh=hh:
                              e.matmul(gb[0:w, 0:512],
                                       wgb[pr][:, w * kc:w * (kc + 1)],
                                       xn_fm[:, kc * TOWN + 512 * hh:kc * TOWN + 512 * (hh + 1)],
                                       start=(kc == 0), stop=(kc == 31)),
                              inc="PE" if kc == 31 else None, waits=wts)
                for kc in range(32):
                    wts = [tldu, ("VE", t_gu_ev[pr][1])] if kc == 0 else []
                    t_mu = op("tensor", lambda e, kc=kc, w=w, pr=pr, gb=gb, hh=hh:
                              e.matmul(gb[0:w, 512:1024],
                                       wub[pr][:, w * kc:w * (kc + 1)],
                                       xn_fm[:, kc * TOWN + 512 * hh:kc * TOWN + 512 * (hh + 1)],
                                       start=(kc == 0), stop=(kc == 31)),
                              inc="PE" if kc == 31 else None, waits=wts)
                t_wg_cons[pr] = t_mg
                t_wu_cons[pr] = t_mu
                t_si = op("scalar", lambda e, w=w, gb=gb: e.activation(
                    out=hg[0:w, :], in_=gb[0:w, 0:512], func=AF.Silu),
                    inc="AC", waits=[("PE", t_mg), ("VE", t_hm)])
                wts = [("AC", t_si), ("PE", t_mu)]
                if u == 0:
                    wts.append(("PE", t_h_cons))
                t_hm = op("vector", lambda e, w=w, u=u, gb=gb: e.tensor_tensor(
                    out=hhalf[0:w, 512 * u:512 * (u + 1)], in0=hg[0:w, :],
                    in1=gb[0:w, 512:1024], op=ALU.mult), inc="VE", waits=wts)
                t_gu_ev[pr] = (t_si, t_hm)
        t_h_done = t_hm
        for g8 in range(8):
            first = True
            t_dn = None
            for s8 in range(8):
                for ic in range(NIC):
                    w = IW[ic]
                    u = s8 * NIC + ic
                    pr4 = u % 4
                    r0 = ISL * s8 + IOFF[ic]
                    tld = dma("sync", wdb[pr4][0:w, :],
                              wdF[r0:r0 + w, 512 * g8:512 * (g8 + 1)],
                              waits=[("PE", t_wd_cons[pr4]), ("cWD", 1)])
                    last = (s8 == 7 and ic == NIC - 1)
                    for tt in range(4):
                        wts = []
                        if tt == 0:
                            wts = [tld, ("VE", t_h_done)]
                        if first:
                            wts.append(("AC", t_down_ev[tt]))
                        t_dn = op("tensor",
                                  lambda e, w=w, u=u, tt=tt, pr4=pr4, f=first, l=last:
                                  e.matmul(PSW[tt][:, :],
                                           hhalf[0:w, 512 * u + 128 * tt:512 * u + 128 * (tt + 1)],
                                           wdb[pr4][0:w, :], start=f, stop=l),
                                  inc="PE" if tt == 3 else None, waits=wts)
                    t_wd_cons[pr4] = t_dn
                    first = False
            t_h_cons = t_dn
            for tt in range(4):
                pr = tt % 2
                t_gt = op("scalar", lambda e, tt=tt: e.activation(
                    out=gts[:, :], in_=PSW[tt][:, :], func=AF.Silu),
                    inc="AC", waits=[("PE", t_dn), ("VE", t_gts_cons)])
                t_down_ev[tt] = t_gt
                rows0 = 512 * hh + 128 * tt
                tldx = dma("sync", xpb[pr][:, :], X[rows0:rows0 + 128, 512 * g8:512 * (g8 + 1)],
                           waits=[("VE", t_xp_cons[pr])])
                tlds = dma("sync", spb[pr][:, :],
                           s_store[rows0:rows0 + 128, 512 * g8:512 * (g8 + 1)],
                           waits=[("VE", t_sp_cons[pr])] + t_s_done)
                op("vector", lambda e, pr=pr: e.tensor_tensor(
                    out=gts[:, :], in0=spb[pr][:, :], in1=gts[:, :], op=ALU.mult),
                    waits=[("AC", t_gt), tlds, t_ou_dma[pr]])
                t_ou = op("vector", lambda e, pr=pr: e.tensor_tensor(
                    out=oub[pr][:, :], in0=gts[:, :], in1=xpb[pr][:, :], op=ALU.add),
                    inc="VE", waits=[tldx])
                t_gts_cons = t_ou
                t_xp_cons[pr] = t_ou
                t_sp_cons[pr] = t_ou
                t_ou_dma[pr] = dma("scalar", OUT[rows0:rows0 + 128, 512 * g8:512 * (g8 + 1)],
                                   oub[pr][:, :], waits=[("VE", t_ou)])

    if dbg:
        wend("sync", dma_now("ST") + dma_now("LD") +
             ([] if nocoll else [(n, 1) for n in (CCNAMES[:4] if nogptq else CCNAMES)]))
        for o, srcap in dbg_outs:
            dma("sync", o, srcap)
        dma("sync", dbg_wgf, wgF[0:DM, :])
        dma("sync", dbg_xnfm, xn_fm[:, :])
        wend("sync", dma_now("LD"))
    return _finish(nc, prog, ctr, sems, wend, es,
                   [] if nocoll else (CCNAMES[:4] if nogptq else CCNAMES))


def _finish(nc, prog, ctr, sems, wend, es, ccnames):
    nc._dbg_ctr = dict(ctr)
    wend("scalar", [(s, v) for s, v in ctr.items() if s.startswith("ST") and v > 0])
    wend("sync", [(s, v) for s, v in ctr.items() if s.startswith("LD") and v > 0])
    wend("gpsimd", [(s, v) for s, v in ctr.items() if s.startswith("GP") and v > 0] +
         [(n, 1) for n in ccnames])

    with nc.Block() as block:
        @block.sync
        def _(e):
            for f in prog["sync"]:
                f(e)

        @block.gpsimd
        def _(e):
            for f in prog["gpsimd"]:
                f(e)

        @block.vector
        def _(e):
            for f in prog["vector"]:
                f(e)

        @block.scalar
        def _(e):
            for f in prog["scalar"]:
                f(e)

        @block.tensor
        def _(e):
            for f in prog["tensor"]:
                f(e)

    es.close()
    return nc


def _in_maps(inputs):
    x = np.asarray(inputs["L0_attn_x"])
    maps = []
    for c in range(NC):
        m = {
            "x_own": np.ascontiguousarray(x[0, c::NC, :], dtype=np.float32),
            "q_codes": inputs["L0_attn_q_codes"],
            "k_codes": inputs["L0_attn_k_codes"],
            "v_codes": inputs["L0_attn_v_codes"],
            "o_codes": inputs["L0_attn_o_codes"],
            "q_scales": inputs["L0_attn_q_scales"],
            "k_scales": inputs["L0_attn_k_scales"],
            "v_scales": inputs["L0_attn_v_scales"],
            "o_scales": inputs["L0_attn_o_scales"],
            "g_qw": np.ascontiguousarray(inputs["L0_ffn_gate_qweight"][:, ISL * c:ISL * (c + 1)]),
            "g_sc": np.ascontiguousarray(inputs["L0_ffn_gate_scales"][:, ISL * c:ISL * (c + 1)]),
            "g_qz": np.ascontiguousarray(inputs["L0_ffn_gate_qzeros"][:, (ISL // 8) * c:(ISL // 8) * (c + 1)]),
            "g_gi": inputs["L0_ffn_gate_g_idx"],
            "u_qw": np.ascontiguousarray(inputs["L0_ffn_up_qweight"][:, ISL * c:ISL * (c + 1)]),
            "u_sc": np.ascontiguousarray(inputs["L0_ffn_up_scales"][:, ISL * c:ISL * (c + 1)]),
            "u_qz": np.ascontiguousarray(inputs["L0_ffn_up_qzeros"][:, (ISL // 8) * c:(ISL // 8) * (c + 1)]),
            "u_gi": inputs["L0_ffn_up_g_idx"],
            "d_qw": np.ascontiguousarray(inputs["L0_ffn_down_qweight"][(ISL // 8) * c:(ISL // 8) * (c + 1), :]),
            "d_sc": inputs["L0_ffn_down_scales"],
            "d_qz": inputs["L0_ffn_down_qzeros"],
            "d_gi": np.ascontiguousarray(inputs["L0_ffn_down_g_idx"][ISL * c:ISL * (c + 1)]),
        }
        m.update(_host_consts(c))
        maps.append({k: np.ascontiguousarray(v) for k, v in m.items()})
    return maps


def kernel(**inputs):
    import os
    nc = build(int(os.environ.get("KUPTO", "99")),
               nogptq=bool(int(os.environ.get("KNOGPTQ", "0"))),
               nocoll=bool(int(os.environ.get("KNOCOLL", "0"))),
               dbg=bool(int(os.environ.get("KDBG", "0"))))
    if int(os.environ.get("KDBG", "0")):
        maps = _in_maps(inputs)
        res = run_bass_kernel_spmd(nc, maps, core_ids=list(range(NC)))
        globals()["_dbg_results"] = res.results[0]
        out = np.empty((1, S, DM), dtype=np.float32)
        for c in range(NC):
            out[0, c::NC, :] = res.results[c]["out_own"]
        return out
    maps = _in_maps(inputs)
    res = run_bass_kernel_spmd(nc, maps, core_ids=list(range(NC)))
    out = np.empty((1, S, DM), dtype=np.float32)
    for c in range(NC):
        out[0, c::NC, :] = res.results[c]["out_own"]
    return out
